# version 06
# baseline (speedup 1.0000x reference)
import numpy as np
import ml_dtypes
import concourse.bass as bass
import concourse.mybir as mybir
from concourse.bass_utils import run_bass_kernel_spmd

F32 = mybir.dt.float32
BF16 = mybir.dt.bfloat16
AF = mybir.ActivationFunctionType
ALU = mybir.AluOpType
AX = mybir.AxisListType

B_, S_, H = 4, 8192, 2048
NH, HD = 16, 128
T = 4096
GT = 512
NG = 8
E, F, FS = 8, 1408, 2816
NFT = 11
EPS = 1e-6
SBUF_BASE = 16384 + 1024
SBUF_BYTES = SBUF_BASE + 205 * 1024

PASSES = ["A1", "A2", "B1", "B2", "P", "B3a", "B3w", "S0", "S1"] + [f"E{i}" for i in range(E)]


class Sem:
    def __init__(self, nc, name):
        self._cm = nc.semaphore(name)
        self.h = self._cm.__enter__()
        self.n = 0


class Prog:
    ENGS = ("tensor", "vector", "scalar", "sync", "gpsimd")

    def __init__(self, nc):
        self.nc = nc
        self.ops = {e: [] for e in self.ENGS}
        self.sbuf_off = SBUF_BASE

    def salloc(self, name, shape, dtype, align=64):
        esz = 2 if dtype == BF16 else 4
        self.sbuf_off = (self.sbuf_off + align - 1) // align * align
        t = self.nc.alloc_sbuf_tensor_at(name, list(shape), dtype, offset=self.sbuf_off)
        self.sbuf_off += int(np.prod(shape[1:])) * esz
        assert self.sbuf_off <= SBUF_BYTES, f"SBUF overflow at {name}: {self.sbuf_off}"
        return t[tuple(slice(None) for _ in shape)]

    def mark(self):
        return self.sbuf_off

    def reset(self, off):
        self.sbuf_off = off

    def on(self, eng, fn, inc=None, k=1):
        incs = []
        if inc is not None:
            incs = inc if isinstance(inc, list) else [(inc, k)]
        if incs:
            def g(EE, f=fn, ii=[(s.h, kk) for s, kk in incs]):
                r = f(EE)
                r.then_inc(*ii[0])
                for h, kk in ii[1:]:
                    EE.nop().then_inc(h, kk)
            self.ops[eng].append(g)
            for s, kk in incs:
                s.n += kk
        else:
            self.ops[eng].append(fn)

    def wait(self, eng, sem, val=None):
        v = sem.n if val is None else val
        if v > 0:
            self.ops[eng].append(lambda EE, s=sem.h, vv=v: EE.wait_ge(s, vv))

    def dma(self, eng, sem, dst, src, **kw):
        def f(EE, d=dst, s=src, i=sem.h, kw=kw):
            EE.dma_start(out=d, in_=s, **kw).then_inc(i, 16)
        self.ops[eng].append(f)
        sem.n += 16

    def dma_tr(self, sem, dst, src):
        def f(EE, d=dst, s=src, i=sem.h):
            EE.dma_start_transpose(d, s).then_inc(i, 16)
        self.ops["sync"].append(f)
        sem.n += 16

    def drain(self, eng, sem, k=1):
        self.ops[eng].append(lambda EE, s=sem.h, kk=k: EE.drain().then_inc(s, kk))
        sem.n += k

    def barrier(self, bs):
        for e in self.ENGS:
            if e in ("vector", "scalar", "gpsimd"):
                self.ops[e].append(lambda EE, s=bs.h: EE.drain().then_inc(s, 1))
            else:
                self.ops[e].append(lambda EE, s=bs.h: EE.nop().then_inc(s, 1))
        bs.n += len(self.ENGS)
        for e in self.ENGS:
            self.ops[e].append(lambda EE, s=bs.h, vv=bs.n: EE.wait_ge(s, vv))

    def emit(self):
        with self.nc.Block() as block:
            @block.tensor
            def _(te):
                for f in self.ops["tensor"]:
                    f(te)

            @block.vector
            def _(ve):
                for f in self.ops["vector"]:
                    f(ve)

            @block.scalar
            def _(se):
                for f in self.ops["scalar"]:
                    f(se)

            @block.sync
            def _(sy):
                for f in self.ops["sync"]:
                    f(sy)

            @block.gpsimd
            def _(gp):
                for f in self.ops["gpsimd"]:
                    f(gp)


def build(upto="ALL", debug=(), b3phases="aceb", b3ng=NG):
    stop_at = len(PASSES) if upto == "ALL" else PASSES.index(upto) + 1
    active = set(PASSES[:stop_at])

    nc = bass.Bass("TRN2", target_bir_lowering=False, debug=False)
    P = Prog(nc)

    din = lambda n, shp, dt=F32: nc.dram_tensor(n, list(shp), dt, kind="ExternalInput").ap()
    x_own = din("x_own", (T, H))
    x_oth = din("x_oth", (T, H))
    wqT = din("wqT", (H, H))
    wkT = din("wkT", (H, H))
    wvT = din("wvT", (H, H))
    woT = din("woT", (H, H))
    eg1T = din("eg1T", (H, 512))
    eg1b = din("eg1b", (512,))
    eg2T = din("eg2T", (512,))
    eg2b = din("eg2b", (1, 1))
    mgate = din("mgate", (1, NH))
    memcat = din("memcat", (HD, NH * 4 * HD))
    normcat = din("normcat", (HD, NH * 4))
    gwT = din("gwT", (H, E))
    mwg = din("mwg", (E, H, F))
    mwu = din("mwu", (E, H, F))
    mwd = din("mwd", (E, F, H))
    swg = din("swg", (H, FS))
    swu = din("swu", (H, FS))
    swd = din("swd", (FS, H))
    c_id_f = din("c_id_f", (128, 128))
    c_triu_f = din("c_triu_f", (128, 128))
    c_ones_f = din("c_ones_f", (128, 130))
    c_premask = din("c_premask", (128, 2))

    out = nc.dram_tensor("out", [T, H], F32, kind="ExternalOutput").ap()

    dscr = lambda n, shp, dt: nc.dram_tensor(n, list(shp), dt, kind="Internal").ap()
    hi_tok = dscr("hi_tok", (2 * T, H), BF16)
    hiT_cm = dscr("hiT_cm", (H, 2 * T), BF16)
    lo_tok = dscr("lo_tok", (T, H), BF16)
    sqT_d = dscr("sqT_d", (H, T), BF16)
    skT_d = dscr("skT_d", (H, 2 * T), BF16)
    v_tok = dscr("v_tok", (2 * T, H), BF16)
    out_pre = dscr("out_pre", (T, H), BF16)
    relscr = dscr("relscr", (4, 16), F32)

    DBG_SPECS = {
        "hi": ((2 * T, H), BF16), "lo": ((T, H), BF16), "sqT": ((H, T), BF16),
        "skT": ((H, 2 * T), BF16), "vtk": ((2 * T, H), BF16), "hiT": ((H, 2 * T), BF16),
        "cw": ((128, 256), F32), "wrow": ((1, 160), F32), "S0": ((128, NH * 128), F32),
        "outT": ((T, H), BF16), "exr": ((1, T), F32), "a1c": ((128, 96), F32),
        "relo": ((4, 16), F32), "memT": ((T, H), BF16), "locT": ((T, H), BF16),
        "gtl": ((128, NH * 128), BF16), "mixT": ((T, H), BF16),
    }
    dbg = {}
    for name in debug:
        shp, dt = DBG_SPECS[name]
        dbg[name] = nc.dram_tensor("dbg_" + name, list(shp), dt, kind="ExternalOutput").ap()
    DBG_MLO = ("memT" in dbg)

    sems = {k: Sem(nc, k) for k in
            ("BAR", "LD", "LDW", "MM", "MM2", "EV", "EV2", "EV3", "ST", "SC", "GP",
             "CV", "CS", "CG")}
    BAR, LD, LDW, MM, MM2, EV, EV2, EV3, ST, SC, GP = (sems[k] for k in
            ("BAR", "LD", "LDW", "MM", "MM2", "EV", "EV2", "EV3", "ST", "SC", "GP"))
    CH = {"vector": sems["CV"], "scalar": sems["CS"], "gpsimd": sems["CG"]}

    def ch(eng, fn, also=None, k=1):
        P.wait(eng, CH[eng])
        inc = [(CH[eng], 1)] + ([(also, k)] if also is not None else [])
        P.on(eng, fn, inc=inc)

    pb = [nc.alloc_psum_tensor(f"pb{i}", [128, 512], F32) for i in range(8)]

    cst = P.salloc("cst", (128, 640), BF16)
    id_bf = cst[:, 0:128]
    triu = cst[:, 128:256]
    ones_col = cst[:, 256:257]
    ones_row = cst[0:1, 258:386]
    wrow_b = cst[0:1, 448:513]
    grow_b = cst[0:1, 520:536]
    cstf = P.salloc("cstf", (128, 160), F32)
    id_f = cstf[:, 0:128]
    premask = cstf[:, 128:129]
    eps_col = cstf[:, 129:130]
    mg_row = cstf[0:1, 130:146]
    cw_all = P.salloc("cw_all", (128, 256), F32)
    wko = P.salloc("wko", (128, 96), F32)
    wcols = wko[:, 0:64]
    zf_col = wko[:, 64:65]
    gcols_f = wko[:, 65:81]
    rows = P.salloc("rows", (128, 256), F32)
    rel_acc = P.salloc("rel_acc", (4, 16), F32)
    S_BASE = P.mark()
    S_fS = P.salloc("S_fS", (128, NH * 128), F32)
    S_fz = P.salloc("S_fz", (128, NH), F32)
    S_bS = P.salloc("S_bS", (128, NH * 128), BF16)
    S_bz = P.salloc("S_bz", (128, NH), BF16)
    PERSIST = P.mark()

    P.dma("sync", LD, cstf[:, 0:128], c_id_f[:, :])
    P.dma("sync", LD, cstf[:, 128:130], c_premask[:, :])
    P.dma("sync", LD, cstf[0:1, 130:146], mgate[:, :])
    P.dma("sync", LD, rows[:, 0:128], c_triu_f[:, :])
    P.wait("vector", LD)
    P.on("vector", lambda ve: ve.tensor_copy(cst[:, 0:128], cstf[:, 0:128]))
    P.on("vector", lambda ve: ve.tensor_copy(cst[:, 128:256], rows[:, 0:128]), inc=EV)
    P.wait("sync", EV)
    P.dma("sync", LD, rows[:, 0:130], c_ones_f[:, :])
    P.wait("vector", LD)
    P.on("vector", lambda ve: ve.tensor_copy(cst[:, 256:258], rows[:, 0:2]))
    P.on("vector", lambda ve: ve.tensor_copy(cst[0:1, 258:386], rows[0:1, 2:130]), inc=EV)
    CONST_LD = LD.n

    def dump(name, src):
        if name in dbg:
            P.barrier(BAR)
            P.wait("sync", ST)
            P.dma("sync", ST, dbg[name][:, :], src)

    if "A1" in active:
        base = P.mark()
        xch = P.salloc("a1_x", (128, 2 * H), F32)
        xnf = P.salloc("a1_xn", (128, H), F32)
        hi_st = P.salloc("a1_hi", (128, 2 * H), BF16)
        lo_st = P.salloc("a1_lo", (128, H), BF16)
        colz = P.salloc("a1_c", (128, 96), F32)

        ev_sn, gp_sn, st_sn = [], [], []
        for i in range(64):
            own = i < 32
            src = x_own if own else x_oth
            r0 = (i % 32) * 128
            buf = xch[:, (i % 2) * H:(i % 2 + 1) * H]
            hbuf = hi_st[:, (i % 2) * H:(i % 2 + 1) * H]
            c = colz[:, (i % 2) * 48:(i % 2) * 48 + 48]
            if i >= 2:
                P.wait("sync", EV, ev_sn[i - 2])
            P.dma("sync", LD, buf, src[r0:r0 + 128, :])
            P.wait("vector", LD)
            P.wait("vector", SC)
            for j in range(4):
                P.on("vector", lambda ve, b=buf, c=c, j=j: ve.bn_stats(
                    c[:, j * 6:(j + 1) * 6], b[:, j * 512:(j + 1) * 512]),
                     inc=CH["vector"] if j == 3 else None)
            mv = c[:, 24:32].rearrange("p (a b) -> p a b", b=2)
            ms = mv[:, :, 0:1].squeeze(2)
            vs = mv[:, :, 1:2].squeeze(2)
            P.wait("vector", CH["vector"])
            for j in range(4):
                P.on("vector", lambda ve, c=c, j=j: ve.bn_aggr(
                    c[:, 24 + 2 * j:26 + 2 * j], c[:, j * 6:(j + 1) * 6]),
                     inc=CH["vector"] if j == 3 else None)
            ch("vector", lambda ve, ms=ms, c=c: ve.reduce_sum(c[:, 32:33], ms, axis=AX.X))
            P.on("vector", lambda ve, vs=vs, c=c: ve.reduce_sum(c[:, 33:34], vs, axis=AX.X))
            P.on("vector", lambda ve, ms=ms, c=c: ve.tensor_mul(c[:, 36:40], ms, ms),
                 inc=CH["vector"])
            ch("vector", lambda ve, c=c: ve.reduce_sum(c[:, 34:35], c[:, 36:40], axis=AX.X))
            ch("vector", lambda ve, c=c: ve.tensor_scalar(
                c[:, 24:25], c[:, 32:33], 0.25, None, op0=ALU.mult))
            ch("vector", lambda ve, c=c: ve.tensor_mul(c[:, 41:42], c[:, 24:25], c[:, 24:25]))
            ch("vector", lambda ve, c=c: ve.tensor_scalar(
                c[:, 42:43], c[:, 33:34], 0.25, None, op0=ALU.mult))
            ch("vector", lambda ve, c=c: ve.tensor_scalar(
                c[:, 43:44], c[:, 34:35], 0.25, None, op0=ALU.mult))
            ch("vector", lambda ve, c=c: ve.tensor_add(c[:, 44:45], c[:, 42:43], c[:, 43:44]))
            ch("vector", lambda ve, c=c: ve.tensor_sub(c[:, 25:26], c[:, 44:45], c[:, 41:42]),
               also=EV2)
            P.wait("scalar", EV2)
            ch("scalar", lambda se, c=c: se.activation(c[:, 26:27], c[:, 25:26], AF.Ln,
                                                       bias=eps_col))
            ch("scalar", lambda se, c=c: se.activation(c[:, 27:28], c[:, 26:27], AF.Exp,
                                                       scale=-0.5), also=SC)
            P.wait("vector", SC)
            ch("vector", lambda ve, c=c: ve.tensor_scalar(
                c[:, 28:29], c[:, 24:25], c[:, 27:28], -1.0, op0=ALU.mult, op1=ALU.mult))
            if own:
                P.wait("vector", GP)
            if i >= 2:
                P.wait("vector", ST, st_sn[i - 2])
            ch("vector", lambda ve, b=buf, hb=hbuf, c=c: ve.tensor_scalar(
                hb, b, c[:, 27:28], c[:, 28:29], op0=ALU.mult, op1=ALU.add), also=EV)
            P.drain("vector", EV)
            if own:
                P.on("vector", lambda ve, b=buf, c=c: ve.tensor_scalar(
                    xnf, b, c[:, 27:28], c[:, 28:29], op0=ALU.mult, op1=ALU.add), inc=EV)
                P.wait("gpsimd", EV)
                P.wait("gpsimd", ST)
                P.on("gpsimd", lambda gp, hb=hbuf: gp.tensor_sub(lo_st[:, :], xnf, hb))
                P.drain("gpsimd", GP)
                P.wait("sync", GP)
                P.dma("sync", ST, lo_tok[r0:r0 + 128, :], lo_st[:, :])
            P.wait("sync", EV)
            dr = r0 if own else T + r0
            P.dma("sync", ST, hi_tok[dr:dr + 128, :], hbuf)
            ev_sn.append(EV.n)
            gp_sn.append(GP.n)
            st_sn.append(ST.n)
        P.barrier(BAR)
        if "a1c" in dbg:
            P.dma("sync", ST, dbg["a1c"][:, :], colz)
        P.reset(base)
        P.barrier(BAR)
        dump("hi", hi_tok[:, :])
        dump("lo", lo_tok[:, :])

    def _nf_rel(h, sq_s, nrm_s, ev3_sn):
        bank = pb[2 + (h % 2)]
        P.wait("tensor", EV)
        if h >= 2:
            P.wait("tensor", EV3, ev3_sn[h - 2])
        P.on("tensor", lambda te, o=bank[0:4, :], l=nrm_s[:, h * 4:(h + 1) * 4],
             r=sq_s[:, h * GT:(h + 1) * GT]: te.matmul(o, l, r, start=True, stop=True),
             inc=MM2)
        P.wait("vector", MM2)
        ch("vector", lambda ve, b=bank: ve.reduce_sum(rows[0:4, 254:255], b[0:4, :],
                                                      axis=AX.X))
        ch("vector", lambda ve, h=h: ve.tensor_add(rel_acc[:, h:h + 1],
                                                   rel_acc[:, h:h + 1],
                                                   rows[0:4, 254:255]), also=EV3)
        ev3_sn.append(EV3.n)

    def proj_pass(kind):
        base = P.mark()
        w_s = P.salloc(f"{kind}_w", (128, 16 * H), BF16)
        hiT_s = P.salloc(f"{kind}_hiT", (128, 2 * 16 * GT), BF16)
        sq_s = P.salloc(f"{kind}_sq", (128, 16 * GT), BF16)
        ew = P.salloc(f"{kind}_ew", (128, 6 * GT), F32)
        if kind == "q":
            nrm_s = P.salloc("a2_nrm", (128, NH * 4), BF16)
            gw_s = P.salloc("a2_gw", (128, 256), BF16)
            gw_f = P.salloc("a2_gwf", (128, 128), F32)
            loT_s = P.salloc("a2_loT", (128, 16 * GT), BF16)
            lgt = P.salloc("a2_lg", (128, 1024), F32)
            lgrow = lgt[0:8, 0:512]
            lgtok = lgt[:, 512:544]
            tk = lgt[:, 544:576]

        wsrc = wqT if kind == "q" else wkT
        P.wait("sync", ST)
        for ct in range(16):
            P.dma("gpsimd", LDW, w_s[:, ct * H:(ct + 1) * H], wsrc[ct * 128:(ct + 1) * 128, :])
        if kind == "q":
            P.dma("gpsimd", LDW, nrm_s[:, :], normcat[:, :])
            P.dma("gpsimd", LDW, gw_f.rearrange("p (a e) -> p a e", e=8),
                  gwT.rearrange("(a p) e -> p a e", p=128))
            P.wait("vector", LDW)
            ch("vector", lambda ve: ve.tensor_copy(gw_s[:, 0:128], gw_f[:, :]))
            ch("vector", lambda ve: ve.tensor_sub(gw_f[:, :], gw_f[:, :], gw_s[:, 0:128]))
            ch("vector", lambda ve: ve.tensor_copy(gw_s[:, 128:256], gw_f[:, :]))
            P.drain("vector", EV3)
            P.on("vector", lambda ve: ve.memset(rel_acc[:, :], 0.0), inc=EV3)
        P.wait("tensor", LDW)

        mm_sn = {}
        ev_sn = []
        st_sn = {}
        for g in range(16):
            own = g < 8
            t0 = (g * GT) if own else (T + (g - 8) * GT)
            hbuf = hiT_s[:, (g % 2) * 16 * GT:(g % 2 + 1) * 16 * GT]
            if g >= 2:
                P.wait("sync", MM, mm_sn[g - 2])
                if kind == "q":
                    P.wait("sync", ST, st_sn[g - 2])
            if kind == "q":
                for ct in range(16):
                    P.dma_tr(LD, hbuf[:, ct * GT:(ct + 1) * GT],
                             hi_tok[t0:t0 + GT, ct * 128:(ct + 1) * 128])
            else:
                P.dma("sync", LD, hbuf.rearrange("p (ct t) -> p ct t", t=GT),
                      hiT_cm[:, t0:t0 + GT].rearrange("(ct p) t -> p ct t", p=128))
            if kind == "q":
                P.wait("sync", LD)
                P.dma("sync", ST, hiT_cm[:, t0:t0 + GT].rearrange(
                    "(ct p) t -> p ct t", p=128), hbuf.rearrange("p (ct t) -> p ct t", t=GT))
                st_sn[g] = ST.n
            if kind == "q" and own:
                P.wait("sync", MM)
                for ct in range(16):
                    P.dma_tr(LD, loT_s[:, ct * GT:(ct + 1) * GT],
                             lo_tok[g * GT:g * GT + GT, ct * 128:(ct + 1) * 128])
            P.wait("tensor", LD)
            if kind == "q" and g == 0:
                P.wait("tensor", EV3)

            if kind == "q" and own:
                pairs = [(gw_s[:, 0:128], hbuf), (gw_s[:, 128:256], hbuf),
                         (gw_s[:, 0:128], loT_s)]
                nmm = 0
                for pi, (gsl, xs) in enumerate(pairs):
                    for ct in range(16):
                        last = (pi == 2 and ct == 15)
                        P.on("tensor", lambda te, o=pb[4][0:8, :],
                             l=gsl[:, ct * 8:(ct + 1) * 8],
                             r=xs[:, ct * GT:(ct + 1) * GT],
                             st=(nmm == 0), sp=last: te.matmul(o, l, r, start=st, stop=sp),
                             inc=MM if last else None)
                        nmm += 1
                P.wait("vector", MM)
                P.on("vector", lambda ve: ve.tensor_copy(lgrow, pb[4][0:8, :]))
                P.drain("vector", EV2)
                P.wait("tensor", EV2)
                for sb in range(4):
                    P.on("tensor", lambda te, sb=sb: te.transpose(
                        pb[5][:, sb * 8:sb * 8 + 8], lgrow[0:8, sb * 128:(sb + 1) * 128],
                        id_f[0:8, 0:8]), inc=MM if sb == 3 else None)
                P.wait("vector", MM)
                ch("vector", lambda ve: ve.tensor_copy(lgtok, pb[5][:, 0:32]), also=EV2)
                for sb in range(4):
                    lgq = lgtok[:, sb * 8:(sb + 1) * 8]
                    cwo = cw_all[:, (g * 4 + sb) * 8:(g * 4 + sb) * 8 + 8]
                    ch("vector", lambda ve, q=lgq: ve.max(out=tk[:, 0:8], in_=q))
                    ch("vector", lambda ve: ve.tensor_scalar(
                        tk[:, 8:9], tk[:, 0:1], -1.0, None, op0=ALU.mult))
                    ch("vector", lambda ve, q=lgq: ve.tensor_scalar(
                        tk[:, 16:24], q, tk[:, 1:2], None, op0=ALU.is_ge), also=EV2)
                    P.wait("scalar", EV2)
                    P.on("scalar", lambda se, q=lgq: se.activation(
                        tk[:, 24:32], q, AF.Exp, bias=tk[:, 8:9]), inc=SC)
                    P.wait("vector", SC)
                    ch("vector", lambda ve: ve.tensor_mul(tk[:, 24:32], tk[:, 24:32],
                                                          tk[:, 16:24]))
                    ch("vector", lambda ve: ve.reduce_sum(tk[:, 9:10], tk[:, 24:32],
                                                          axis=AX.X))
                    ch("vector", lambda ve: ve.reciprocal(tk[:, 10:11], tk[:, 9:10]))
                    ch("vector", lambda ve, o=cwo: ve.tensor_scalar(
                        o, tk[:, 24:32], tk[:, 10:11], None, op0=ALU.mult), also=EV2)

            ev3_sn = []
            for dt in range(16):
                bank = pb[dt % 2]
                if len(ev_sn) >= 2:
                    P.wait("tensor", EV, ev_sn[-2])
                for ct in range(16):
                    P.on("tensor", lambda te, o=bank[:, :],
                         l=w_s[:, ct * H + dt * 128:ct * H + (dt + 1) * 128],
                         r=hbuf[:, ct * GT:(ct + 1) * GT],
                         st=(ct == 0), sp=(ct == 15): te.matmul(o, l, r, start=st, stop=sp),
                         inc=MM if ct == 15 else None)
                sl = ew[:, (dt % 2) * 3 * GT:(dt % 2) * 3 * GT + 3 * GT]
                P.wait("vector", MM)
                P.on("vector", lambda ve, b=bank, s=sl: ve.tensor_scalar(
                    s[:, 0:GT], b[:, :], 0.0, None, op0=ALU.min), inc=EV2)
                P.wait("scalar", EV2)
                P.on("scalar", lambda se, s=sl: se.activation(
                    s[:, GT:2 * GT], s[:, 0:GT], AF.Exp), inc=SC)
                P.wait("gpsimd", SC)
                P.on("gpsimd", lambda gp, s=sl: gp.tensor_sub(
                    s[:, 2 * GT:3 * GT], s[:, GT:2 * GT], s[:, 0:GT]), inc=GP)
                P.wait("vector", GP)
                P.on("vector", lambda ve, b=bank, s=sl, o=sq_s[:, dt * GT:(dt + 1) * GT]:
                     ve.scalar_tensor_tensor(o, b[:, :], 1.0, s[:, 2 * GT:3 * GT],
                                             op0=ALU.mult, op1=ALU.add), inc=EV)
                ev_sn.append(EV.n)
                if kind == "q" and dt >= 1:
                    _nf_rel(dt - 1, sq_s, nrm_s, ev3_sn)
            if kind == "q":
                _nf_rel(15, sq_s, nrm_s, ev3_sn)
            P.drain("vector", EV)
            P.wait("sync", EV)
            if kind == "q":
                if own:
                    P.dma("sync", ST, sqT_d[:, g * GT:g * GT + GT].rearrange(
                        "(dt p) t -> p dt t", p=128),
                        sq_s.rearrange("p (dt t) -> p dt t", t=GT))
            else:
                dst0 = g * GT if own else T + (g - 8) * GT
                P.dma("sync", ST, skT_d[:, dst0:dst0 + GT].rearrange(
                    "(dt p) t -> p dt t", p=128),
                    sq_s.rearrange("p (dt t) -> p dt t", t=GT))
            mm_sn[g] = MM.n
            P.wait("vector", ST)
        P.reset(base)

    if "A2" in active:
        P.barrier(BAR)
        proj_pass("q")
        nrmA = P.salloc("a2_nrmA", (128, 64), BF16)
        P.drain("vector", EV3)
        P.wait("sync", EV3)
        P.dma("sync", ST, relscr[:, :], rel_acc[:, :])
        P.wait("sync", ST)
        P.dma("sync", LD, rows[0:1, 0:64], relscr[:, :].transpose([1, 0]),
              allow_slow_non_contiguous=True)
        r3 = rows[0:1, 0:64].rearrange("p (h m) -> p h m", m=4)
        mx = rows[0:1, 64 + 16:64 + 32]
        sm = rows[0:1, 96:112]
        P.wait("vector", LD)
        ch("vector", lambda ve: ve.tensor_scalar(rows[0:1, 0:64], rows[0:1, 0:64],
                                                 1.0 / S_, None, op0=ALU.mult))
        ch("vector", lambda ve: ve.tensor_reduce(mx, r3, axis=AX.X, op=ALU.max))
        ch("vector", lambda ve: ve.tensor_tensor(
            r3, r3, mx.unsqueeze(2).to_broadcast((1, 16, 4)),
            op=ALU.subtract), also=EV2)
        P.wait("scalar", EV2)
        P.on("scalar", lambda se: se.activation(rows[0:1, 0:64], rows[0:1, 0:64], AF.Exp),
             inc=SC)
        P.wait("vector", SC)
        ch("vector", lambda ve: ve.tensor_reduce(sm, r3, axis=AX.X, op=ALU.add))
        ch("vector", lambda ve: ve.reciprocal(rows[0:1, 112:128], sm))
        ch("vector", lambda ve: ve.tensor_tensor(
            r3, r3, rows[0:1, 112:128].unsqueeze(2).to_broadcast((1, 16, 4)),
            op=ALU.mult), also=EV)
        P.dma("gpsimd", LDW, nrmA[:, :], normcat[:, :])
        P.wait("tensor", LDW)
        for h in range(16):
            P.on("tensor", lambda te, h=h: te.matmul(pb[6][0:1, 0:4], ones_col,
                                                     nrmA[:, h * 4:(h + 1) * 4],
                                                     start=(h == 0), stop=(h == 15)),
                 inc=MM if h == 15 else None)
        P.wait("vector", MM)
        ch("vector", lambda ve: ve.tensor_copy(rows[0:1, 128:132], pb[6][0:1, 0:4]))
        ch("vector", lambda ve: ve.reduce_sum(rows[0:1, 132:133], rows[0:1, 128:132],
                                              axis=AX.X))
        ch("vector", lambda ve: ve.tensor_scalar(rows[0:1, 136:140], rows[0:1, 128:132],
                                                 EPS, None, op0=ALU.is_ge))
        ch("vector", lambda ve: ve.tensor_scalar(rows[0:1, 64:65], rows[0:1, 132:133],
                                                 EPS, None, op0=ALU.is_ge))
        ch("vector", lambda ve: ve.tensor_tensor(
            r3, r3, rows[0:1, 136:140].unsqueeze(1).to_broadcast((1, 16, 4)),
            op=ALU.mult))
        ch("vector", lambda ve: ve.tensor_copy(wrow_b, rows[0:1, 0:65]), also=EV)
        P.wait("scalar", LD, CONST_LD)
        P.on("scalar", lambda se: se.activation(rows[0:1, 144:160], mg_row, AF.Sigmoid),
             inc=SC)
        P.wait("vector", SC)
        ch("vector", lambda ve: ve.tensor_copy(grow_b, rows[0:1, 144:160]))
        P.drain("vector", EV)
        P.wait("tensor", EV)
        P.on("tensor", lambda te: te.matmul(pb[6][:, 0:65], ones_row, wrow_b,
                                            start=True, stop=True))
        P.on("tensor", lambda te: te.matmul(pb[7][:, 0:16], ones_row, grow_b,
                                            start=True, stop=True), inc=MM)
        P.wait("vector", MM)
        P.on("vector", lambda ve: ve.tensor_copy(wcols, pb[6][:, 0:64]))
        P.on("vector", lambda ve: ve.tensor_copy(zf_col, pb[6][:, 64:65]))
        P.on("vector", lambda ve: ve.tensor_copy(gcols_f, pb[7][:, 0:16]), inc=EV)
        P.barrier(BAR)
        dump("sqT", sqT_d[:, :])
        dump("hiT", hiT_cm[:, :])
        if "cw" in dbg:
            P.dma("sync", ST, dbg["cw"][:, :], cw_all[:, :])
        if "wrow" in dbg:
            P.dma("sync", ST, dbg["wrow"][0:1, 0:96], wko[0:1, 0:96])
            P.dma("sync", ST, dbg["wrow"][0:1, 96:160], rows[0:1, 0:64])
        if "relo" in dbg:
            P.dma("sync", ST, dbg["relo"][:, :], relscr[:, :])

    if "B1" in active:
        P.barrier(BAR)
        proj_pass("k")
        P.barrier(BAR)
        dump("skT", skT_d[:, :])

    if "B2" in active:
        P.barrier(BAR)
        base = P.mark()
        wv_s = P.salloc("b2_wv", (128, 16 * H), BF16)
        hiT_s = P.salloc("b2_hiT", (128, 2 * 16 * GT), BF16)
        vst = P.salloc("b2_vst", (128, 2 * H), BF16)
        P.wait("sync", ST)
        for ct in range(16):
            P.dma("gpsimd", LDW, wv_s[:, ct * H:(ct + 1) * H], wvT[ct * 128:(ct + 1) * 128, :])
        P.wait("tensor", LDW)
        mm_sn = {}
        ev_sn2 = []
        st_sn2 = []
        for g in range(16):
            own = g < 8
            t0 = (g * GT) if own else (T + (g - 8) * GT)
            hbuf = hiT_s[:, (g % 2) * 16 * GT:(g % 2 + 1) * 16 * GT]
            if g >= 2:
                P.wait("sync", MM, mm_sn[g - 2])
            P.dma("sync", LD, hbuf.rearrange("p (ct t) -> p ct t", t=GT),
                  hiT_cm[:, t0:t0 + GT].rearrange("(ct p) t -> p ct t", p=128))
            P.wait("tensor", LD)
            for ts_ in range(4):
                vbuf = vst[:, (ts_ % 2) * H:(ts_ % 2 + 1) * H]
                if len(st_sn2) >= 2:
                    P.wait("vector", ST, st_sn2[-2])
                for oc in range(4):
                    bank = pb[oc % 2]
                    if len(ev_sn2) >= 2:
                        P.wait("tensor", EV, ev_sn2[-2])
                    for ct in range(16):
                        P.on("tensor", lambda te, o=bank[:, :],
                             l=hbuf[:, ct * GT + ts_ * 128:ct * GT + (ts_ + 1) * 128],
                             r=wv_s[:, ct * H + oc * 512:ct * H + (oc + 1) * 512],
                             st=(ct == 0), sp=(ct == 15):
                             te.matmul(o, l, r, start=st, stop=sp),
                             inc=MM if ct == 15 else None)
                    P.wait("vector", MM)
                    P.on("vector", lambda ve, b=bank, o=vbuf[:, oc * 512:(oc + 1) * 512]:
                         ve.tensor_copy(o, b[:, :]), inc=EV)
                    ev_sn2.append(EV.n)
                P.drain("vector", EV)
                P.wait("sync", EV)
                P.dma("sync", ST, v_tok[t0 + ts_ * 128:t0 + (ts_ + 1) * 128, :], vbuf)
                st_sn2.append(ST.n)
            mm_sn[g] = MM.n
        P.reset(base)
        P.barrier(BAR)
        dump("vtk", v_tok[:, :])

    if "P" in active:
        P.barrier(BAR)
        base = P.mark()
        skc_s = P.salloc("p_skc", (128, 2 * 4096), BF16)
        vc_s = P.salloc("p_vc", (128, 2 * 4096), BF16)
        P.wait("sync", ST)
        mm_sn = {}
        for h in range(NH):
            sk_sl = skc_s[:, (h % 2) * 4096:(h % 2 + 1) * 4096]
            vc_sl = vc_s[:, (h % 2) * 4096:(h % 2 + 1) * 4096]
            if h >= 2:
                P.wait("sync", MM, mm_sn[h - 2])
            for c in range(32):
                P.dma_tr(LD, sk_sl[:, c * 128:(c + 1) * 128],
                         skT_d[h * 128:(h + 1) * 128, T + c * 128:T + (c + 1) * 128])
            P.dma("sync", LD, vc_sl.rearrange("p (c e) -> p c e", e=128),
                  v_tok[T:2 * T, h * 128:(h + 1) * 128].rearrange("(c p) e -> p c e", p=128))
            bank = pb[h % 2]
            P.wait("tensor", LD)
            P.wait("tensor", EV)
            for c in range(32):
                P.on("tensor", lambda te, o=bank[:, 0:128],
                     l=sk_sl[:, c * 128:(c + 1) * 128], r=vc_sl[:, c * 128:(c + 1) * 128],
                     st=(c == 0), sp=(c == 31): te.matmul(o, l, r, start=st, stop=sp))
                P.on("tensor", lambda te, o=bank[:, 128:129],
                     l=sk_sl[:, c * 128:(c + 1) * 128], r=ones_col,
                     sp=(c == 31): te.matmul(o, l, r, start=False, stop=sp,
                                             skip_group_check=True),
                     inc=MM if c == 31 else None)
            mm_sn[h] = MM.n
            P.wait("vector", MM)
            P.on("vector", lambda ve, b=bank, h=h: ve.tensor_scalar(
                S_fS[:, h * 128:(h + 1) * 128], b[:, 0:128], premask[:, 0:1], None,
                op0=ALU.mult))
            P.on("vector", lambda ve, b=bank, h=h: ve.tensor_scalar(
                S_fz[:, h:h + 1], b[:, 128:129], premask[:, 0:1], None,
                op0=ALU.mult))
            P.drain("vector", EV)
            P.wait("scalar", EV)
            P.on("scalar", lambda se, h=h: se.copy(S_bS[:, h * 128:(h + 1) * 128],
                                                   S_fS[:, h * 128:(h + 1) * 128]))
            P.on("scalar", lambda se, h=h: se.copy(S_bz[:, h:h + 1],
                                                   S_fz[:, h:h + 1]), inc=SC)
        P.reset(base)
        P.barrier(BAR)
        dump("S0", S_fS[:, :])

    if "B3a" in active:
        P.barrier(BAR)
        base = P.mark()
        mem_s = P.salloc("b3_mem", (128, NH * 512), BF16)
        Mh_s = P.salloc("b3_Mh", (128, NH * 128), BF16)
        nrm_s = P.salloc("b3_nrm", (128, NH * 4), BF16)
        eg1_s = P.salloc("b3_eg1", (128, 16 * 512), BF16)
        eg1b_s = P.salloc("b3_eg1b", (128, 4), F32)
        eg2_s = P.salloc("b3_eg2", (128, 4), BF16)
        eg2b_s = P.salloc("b3_eg2b", (1, 1), F32)
        sq_g = P.salloc("b3_sq", (128, 16 * GT), BF16)
        sk_g = P.salloc("b3_sk", (128, 16 * GT), BF16)
        v_g = P.salloc("b3_v", (128, 2 * 4 * H), BF16)
        skc_g = P.salloc("b3_skc", (128, 2 * 8192), BF16)
        coaT = P.salloc("b3_coaT", (128, 16 * GT), BF16)
        L1T = P.salloc("b3_L1T", (128, 4 * GT), BF16)
        otK = P.salloc("b3_otK", (128, 2 * H), BF16)
        nf_all = P.salloc("b3_nf", (128, 256), F32)
        ucp_all = P.salloc("b3_ucp", (128, 64), F32)
        exr_row = P.salloc("b3_exr", (128, 512), F32)
        exc_s = P.salloc("b3_exc", (128, 8), F32)
        sh1 = P.salloc("b3_sh1", (128, 2048), BF16)
        mscb = P.salloc("b3_mscb", (128, 2048), BF16)
        locb = P.salloc("b3_locb", (128, 2048), BF16)
        mtmp = P.salloc("b3_mtmp", (128, 1024), BF16)
        igcols = P.salloc("b3_ig", (128, 16), F32)
        u4a = P.salloc("b3_u4a", (128, 2 * 64), F32)
        ucm_a = P.salloc("b3_ucm", (128, 16), F32)
        dcol = P.salloc("b3_dc", (128, 32), F32)
        epsc = P.salloc("b3_eps", (128, 1), F32)
        triu4 = P.salloc("b3_tr4", (128, 512), BF16)

        P.wait("sync", ST)
        P.dma("gpsimd", LDW, mem_s[:, :], memcat[:, :])
        P.on("vector", lambda ve: ve.tensor_scalar(
            epsc[:, :], zf_col, 0.0, EPS, op0=ALU.mult, op1=ALU.add))
        P.dma("gpsimd", LDW, nrm_s[:, :], normcat[:, :])
        for ct in range(16):
            P.dma("gpsimd", LDW, eg1_s[:, ct * 512:(ct + 1) * 512],
                  eg1T[ct * 128:(ct + 1) * 128, :])
        P.dma("gpsimd", LDW, eg1b_s[:, :], eg1b.rearrange("(a p) -> p a", p=128),
              allow_slow_non_contiguous=True)
        P.dma("gpsimd", LDW, eg2_s[:, :], eg2T.rearrange("(a p) -> p a", p=128),
              allow_slow_non_contiguous=True)
        P.dma("gpsimd", LDW, eg2b_s[:, :], eg2b[:, :])
        ch("vector", lambda ve: ve.tensor_scalar(
            igcols[:, :], gcols_f, -1.0, 1.0, op0=ALU.mult, op1=ALU.add))
        for j in range(4):
            ch("vector", lambda ve, j=j: ve.tensor_copy(
                triu4[:, j * 128:(j + 1) * 128], triu))
        P.drain("vector", EV)
        P.wait("gpsimd", LDW)
        for h in range(NH):
            m = mem_s[:, h * 512:(h + 1) * 512]
            o = Mh_s[:, h * 128:(h + 1) * 128]
            ch("gpsimd", lambda gp, m=m, o=o: gp.tensor_add(o, m[:, 0:128], m[:, 128:256]))
            P.on("gpsimd", lambda gp, m=m: gp.tensor_add(L1T[:, 0:128], m[:, 256:384],
                                                         m[:, 384:512]), inc=CH["gpsimd"])
            ch("gpsimd", lambda gp, o=o: gp.tensor_add(o, o, L1T[:, 0:128]))
        P.drain("gpsimd", GP)
        P.wait("tensor", GP)
        P.wait("tensor", EV)
        P.wait("tensor", LDW)

        def b_u4(c):
            u4p = u4a[:, (c % 2) * 64:(c % 2 + 1) * 64]
            nfv = nf_all.rearrange("p (hh s m) -> p hh s m", s=4, m=4)[:, :, c, :]
            u43 = u4p.rearrange("p (hh m) -> p hh m", m=4)
            ch("vector", lambda ve, a=nfv, o=u43: ve.tensor_tensor(
                o, a, epsc.unsqueeze(1).to_broadcast((128, 16, 4)), op=ALU.max))
            ch("vector", lambda ve, o=u4p: ve.reciprocal(o, o))
            ch("vector", lambda ve, o=u4p: ve.tensor_tensor(o, o, wcols, op=ALU.mult))
            ch("vector", lambda ve, o=u4p, c=c: ve.tensor_scalar(
                o, o, exc_s[:, c:c + 1], None, op0=ALU.mult))
            ucpv = ucp_all.rearrange("p (hh s) -> p hh s", s=4)[:, :, c:c + 1]
            ch("vector", lambda ve, u=ucpv, c=c: ve.tensor_tensor(
                ucm_a[:, 0:16].rearrange("p (hh o) -> p hh o", o=1),
                u, exc_s[:, 4 + c:5 + c].unsqueeze(1).to_broadcast((128, 16, 1)),
                op=ALU.mult))
            ch("vector", lambda ve, o=u43: ve.tensor_tensor(
                o, o, ucm_a[:, 0:16].rearrange("p (hh o) -> p hh o", o=1)
                .to_broadcast((128, 16, 4)), op=ALU.add))
            ch("vector", lambda ve, o=u43: ve.tensor_tensor(
                o, o, gcols_f.rearrange("p (hh o) -> p hh o", o=1)
                .to_broadcast((128, 16, 4)), op=ALU.mult))

        last_spill = {0: 0, 1: 0}
        sc_pb03 = 0
        gp_read_done = 0
        gp_lane_free = 0
        u4free = {0: 0, 1: 0}
        sc_H = 0
        mm_g_end = {}
        uv3 = ucp_all.rearrange("p (h s) -> p h s", s=4)
        for g in range(NG):
            t0 = g * GT
            vb = v_g[:, (g % 2) * 4 * H:(g % 2 + 1) * 4 * H]
            skcb = skc_g[:, (g % 2) * 8192:(g % 2 + 1) * 8192]
            if g >= 2:
                P.wait("sync", MM, mm_g_end[g - 2])
            for c in range(4):
                P.dma("sync", LD, vb[:, c * H:(c + 1) * H],
                      v_tok[t0 + c * 128:t0 + (c + 1) * 128, :])
            ld_v = LD.n
            for h in range(NH):
                for c in range(4):
                    P.dma_tr(LD, skcb[:, (h * 4 + c) * 128:(h * 4 + c + 1) * 128],
                             skT_d[h * 128:(h + 1) * 128, t0 + c * 128:t0 + (c + 1) * 128])
            ld_skc = LD.n
            P.wait("sync", MM)
            P.dma("sync", LD, sq_g.rearrange("p (dt t) -> p dt t", t=GT),
                  sqT_d[:, t0:t0 + GT].rearrange("(dt p) t -> p dt t", p=128))
            P.dma("sync", LD, sk_g.rearrange("p (dt t) -> p dt t", t=GT),
                  skT_d[:, t0:t0 + GT].rearrange("(dt p) t -> p dt t", p=128))
            ld_qk = LD.n

            P.wait("tensor", LD, ld_qk)
            P.wait("tensor", EV)
            P.wait("tensor", SC)
            P.wait("tensor", GP)
            first = True
            for sb in range(4):
                for h in range(NH):
                    o = pb[4][:, (h * 4 + sb) * 4:(h * 4 + sb) * 4 + 4]
                    l = sq_g[:, h * GT + sb * 128:h * GT + (sb + 1) * 128]
                    r = nrm_s[:, h * 4:(h + 1) * 4]
                    P.on("tensor", lambda te, o=o, l=l, r=r, st=first:
                         te.matmul(o, l, r, start=st, stop=True,
                                   skip_group_check=not st),
                         inc=MM if (sb == 3 and h == NH - 1) else None)
                    first = False
            P.wait("vector", MM)
            P.wait("vector", GP)
            ch("vector", lambda ve: ve.tensor_copy(nf_all[:, :], pb[4][:, 0:256]))
            ch("vector", lambda ve: ve.tensor_reduce(
                ucp_all[:, :], nf_all.rearrange("p (a m) -> p a m", m=4),
                axis=AX.X, op=ALU.add))
            ch("vector", lambda ve: ve.tensor_scalar(
                ucp_all[:, :], ucp_all[:, :], EPS, None, op0=ALU.max))
            ch("vector", lambda ve: ve.reciprocal(ucp_all[:, :], ucp_all[:, :]))
            ch("vector", lambda ve: ve.tensor_scalar(
                ucp_all[:, :], ucp_all[:, :], zf_col, None, op0=ALU.mult))
            chv_ucp = CH["vector"].n

            sc_coa = []
            tr_rot = 0
            cost_ev = {}
            for sb in range(4):
                if sb >= 1:
                    P.wait("tensor", EV, cost_ev[sb - 1])
                for b4 in range(4):
                    for hh in range(4):
                        h = b4 * 4 + hh
                        o = pb[b4][:, hh * 128:(hh + 1) * 128]
                        l = sq_g[:, h * GT + sb * 128:h * GT + (sb + 1) * 128]
                        r = Mh_s[:, h * 128:(h + 1) * 128]
                        P.on("tensor", lambda te, o=o, l=l, r=r, st=(hh == 0):
                             te.matmul(o, l, r, start=st, stop=True,
                                       skip_group_check=(hh != 0)),
                             inc=MM if hh == 3 else None)
                if sb == 0:
                    P.wait("vector", CH["vector"], chv_ucp)
                P.wait("vector", MM)
                for b4 in range(4):
                    src = pb[b4][:, 0:512].rearrange("p (h e) -> p h e", e=128)
                    dst = sh1[:, b4 * 512:(b4 + 1) * 512].rearrange(
                        "p (h e) -> p h e", e=128)
                    uv = uv3[:, b4 * 4:(b4 + 1) * 4, sb:sb + 1]
                    P.on("vector", lambda ve, o=dst, a=src, u=uv: ve.tensor_tensor(
                        o, a, u.to_broadcast((128, 4, 128)), op=ALU.mult),
                         inc=EV if b4 == 3 else None)
                cost_ev[sb] = EV.n
                P.wait("tensor", EV, cost_ev[sb])
                for h in range(NH):
                    bank = pb[5 + tr_rot % 2]
                    tr_rot += 1
                    if len(sc_coa) >= 2:
                        P.wait("tensor", SC, sc_coa[-2])
                    P.on("tensor", lambda te, i_=sh1[:, h * 128:(h + 1) * 128],
                         o=bank[:, 0:64].bitcast(BF16)[:, 0:128]:
                         te.transpose(o, i_, id_bf), inc=MM)
                    P.wait("scalar", MM)
                    P.on("scalar", lambda se, h=h, sb=sb,
                         b=bank[:, 0:64].bitcast(BF16)[:, 0:128]:
                         se.copy(coaT[:, h * GT + sb * 128:h * GT + (sb + 1) * 128], b),
                         inc=SC)
                    sc_coa.append(SC.n)

            P.drain("scalar", SC)
            P.wait("tensor", SC)
            P.wait("tensor", EV)
            sc_l1 = []
            for ut in range(4):
                bank = pb[2 + ut % 2]
                if len(sc_l1) >= 2:
                    P.wait("tensor", SC, sc_l1[-2])
                for ct in range(16):
                    P.on("tensor", lambda te, o=bank[:, :],
                         l=eg1_s[:, ct * 512 + ut * 128:ct * 512 + (ut + 1) * 128],
                         r=coaT[:, ct * GT:(ct + 1) * GT],
                         st=(ct == 0), sp=(ct == 15): te.matmul(o, l, r, start=st, stop=sp),
                         inc=MM if ct == 15 else None)
                P.wait("scalar", MM)
                P.on("scalar", lambda se, b=bank, ut=ut: se.activation(
                    L1T[:, ut * 512:(ut + 1) * 512], b[:, :], AF.Relu,
                    bias=eg1b_s[:, ut:ut + 1]), inc=SC)
                sc_l1.append(SC.n)
            P.drain("scalar", SC)
            P.wait("tensor", SC)
            for ut in range(4):
                P.on("tensor", lambda te, ut=ut: te.matmul(
                    pb[4][0:1, :], eg2_s[:, ut:ut + 1], L1T[:, ut * 512:(ut + 1) * 512],
                    start=(ut == 0), stop=(ut == 3)), inc=MM if ut == 3 else None)
            P.wait("scalar", MM)
            P.on("scalar", lambda se: se.activation(exr_row[0:1, :], pb[4][0:1, :],
                                                    AF.Sigmoid, bias=eg2b_s[0:1, 0:1]))
            P.drain("scalar", SC)
            if "exr" in dbg:
                P.wait("sync", SC)
                P.dma("sync", ST, dbg["exr"][0:1, t0:t0 + GT], exr_row[0:1, :])
            P.wait("tensor", SC)
            for sb in range(4):
                P.on("tensor", lambda te, sb=sb: te.transpose(
                    pb[4][:, 64 + sb:64 + sb + 1], exr_row[0:1, sb * 128:(sb + 1) * 128],
                    id_f[0:1, 0:1]), inc=MM if sb == 3 else None)
            P.wait("vector", MM)
            ch("vector", lambda ve: ve.tensor_copy(exc_s[:, 0:4], pb[4][:, 64:68]))
            ch("vector", lambda ve: ve.tensor_scalar(
                exc_s[:, 4:8], exc_s[:, 0:4], -1.0, 1.0, op0=ALU.mult, op1=ALU.add),
                also=EV)
            sc_pb03 = SC.n

            for c in range(4):
                cp = c % 2
                P.wait("vector", SC, u4free[cp])
                b_u4(c)
                chv_u4 = CH["vector"].n
                P.wait("tensor", SC, sc_pb03)
                for h in range(NH):
                    b4, hh = h // 4, h % 4
                    o = pb[b4][:, hh * 128:(hh + 1) * 128]
                    l = sk_g[:, h * GT + c * 128:h * GT + (c + 1) * 128]
                    r = sq_g[:, h * GT + c * 128:h * GT + (c + 1) * 128]
                    P.on("tensor", lambda te, o=o, l=l, r=r, st=(hh == 0):
                         te.matmul(o, l, r, start=st, stop=True,
                                   skip_group_check=(hh != 0)),
                         inc=MM if hh == 3 else None)
                mm_A = MM.n
                P.wait("vector", MM, mm_A)
                for b4 in range(4):
                    P.on("vector", lambda ve, b=pb[b4],
                         o=mscb[:, b4 * 512:(b4 + 1) * 512]:
                         ve.scalar_tensor_tensor(o, b[:, 0:512], 1.0, triu4,
                                                 op0=ALU.mult, op1=ALU.mult),
                         inc=EV if b4 == 3 else None)
                ev_B = EV.n
                P.wait("tensor", EV, ev_B)
                P.wait("tensor", SC, sc_H)
                if c == 0:
                    P.wait("tensor", LD, ld_v)
                for h in range(NH):
                    b4, hh = h // 4, h % 4
                    o = pb[b4][:, hh * 128:(hh + 1) * 128]
                    msct = mscb[:, h * 128:(h + 1) * 128]
                    vct = vb[:, c * H + h * 128:c * H + (h + 1) * 128]
                    sqt = sq_g[:, h * GT + c * 128:h * GT + (c + 1) * 128]
                    P.on("tensor", lambda te, o=o, l=msct, r=vct, st=(hh == 0):
                         te.matmul(o, l, r, start=st, stop=False,
                                   skip_group_check=(hh != 0)))
                    P.on("tensor", lambda te, o=o, l=sqt,
                         S=S_bS[:, h * 128:(h + 1) * 128]:
                         te.matmul(o, l, S, start=False, stop=True,
                                   skip_group_check=True),
                         inc=MM if hh == 3 else None)
                    od = pb[4][:, cp * 16 + h:cp * 16 + h + 1]
                    P.on("tensor", lambda te, o=od, l=msct, st=(h == 0):
                         te.matmul(o, l, ones_col, start=st, stop=False,
                                   skip_group_check=not st))
                    P.on("tensor", lambda te, o=od, l=sqt, z=S_bz[:, h:h + 1]:
                         te.matmul(o, l, z, start=False, stop=True,
                                   skip_group_check=True),
                         inc=MM if h == NH - 1 else None)
                mm_C = MM.n
                P.wait("vector", MM, mm_C)
                ch("vector", lambda ve, cp=cp: ve.tensor_scalar(
                    dcol[:, 0:16], pb[4][:, cp * 16:cp * 16 + 16], EPS, None,
                    op0=ALU.max))
                ch("vector", lambda ve: ve.reciprocal(dcol[:, 16:32], dcol[:, 0:16]))
                ch("vector", lambda ve: ve.tensor_tensor(dcol[:, 0:16], dcol[:, 16:32],
                                                         igcols, op=ALU.mult))
                P.wait("vector", GP, gp_read_done)
                for b4 in range(4):
                    src = pb[b4][:, 0:512].rearrange("p (h e) -> p h e", e=128)
                    dst = locb[:, b4 * 512:(b4 + 1) * 512].rearrange(
                        "p (h e) -> p h e", e=128)
                    u = dcol[:, b4 * 4:(b4 + 1) * 4].unsqueeze(2)
                    f = (lambda ve, o=dst, a=src, u=u: ve.tensor_tensor(
                        o, a, u.to_broadcast((128, 4, 128)), op=ALU.mult))
                    if b4 == 0:
                        ch("vector", f)
                    else:
                        P.on("vector", f, inc=EV if b4 == 3 else None)
                ev_E = EV.n
                P.wait("tensor", EV, ev_E)
                if c == 0:
                    P.wait("tensor", LD, ld_skc)
                for h in range(NH):
                    b4, hh = h // 4, h % 4
                    skct = skcb[:, (h * 4 + c) * 128:(h * 4 + c + 1) * 128]
                    vct = vb[:, c * H + h * 128:c * H + (h + 1) * 128]
                    P.on("tensor", lambda te, o=pb[b4][:, hh * 128:(hh + 1) * 128],
                         l=skct, r=vct, st=(hh == 0):
                         te.matmul(o, l, r, start=st, stop=True,
                                   skip_group_check=(hh != 0)),
                         inc=MM if hh == 3 else None)
                    P.on("tensor", lambda te, o=pb[7][:, cp * 16 + h:cp * 16 + h + 1],
                         l=skct, st=(h == 0):
                         te.matmul(o, l, ones_col, start=st, stop=True,
                                   skip_group_check=not st),
                         inc=MM if h == NH - 1 else None)
                mm_F = MM.n
                P.wait("vector", MM, mm_F)
                P.wait("vector", SC, sc_H)
                for b4 in range(4):
                    P.on("vector", lambda ve, s=S_fS[:, b4 * 512:(b4 + 1) * 512],
                         b=pb[b4]: ve.tensor_add(s, s, b[:, 0:512]))
                P.on("vector", lambda ve, cp=cp: ve.tensor_add(
                    S_fz[:, :], S_fz[:, :], pb[7][:, cp * 16:cp * 16 + 16]), inc=EV)
                ev_G = EV.n
                P.wait("scalar", EV, ev_G)
                P.on("scalar", lambda se: se.copy(S_bS[:, 0:1024], S_fS[:, 0:1024]))
                P.on("scalar", lambda se: se.copy(S_bS[:, 1024:2048], S_fS[:, 1024:2048]))
                P.on("scalar", lambda se: se.copy(S_bz[:, :], S_fz[:, :]))
                P.drain("scalar", SC)
                sc_H = SC.n
                P.wait("scalar", CH["vector"], chv_u4)
                sc_grp = {}
                gp_grp = {}
                for k in range(4):
                    if k == 0:
                        P.wait("tensor", EV, ev_G)
                    else:
                        P.wait("tensor", SC, sc_grp[k - 1])
                    for hh in range(4):
                        h = k * 4 + hh
                        sqt = sq_g[:, h * GT + c * 128:h * GT + (c + 1) * 128]
                        P.on("tensor", lambda te, o=pb[hh][:, :], l=sqt,
                             r=mem_s[:, h * 512:(h + 1) * 512]:
                             te.matmul(o, l, r, start=True, stop=True),
                             inc=MM if hh == 3 else None)
                    mm_af = MM.n
                    P.wait("scalar", MM, mm_af)
                    P.wait("scalar", GP, gp_lane_free)
                    for hh in range(4):
                        h = k * 4 + hh
                        u4base = cp * 64 + h * 4
                        for m in range(4):
                            P.on("scalar", lambda se,
                                 o=sh1[:, hh * 512 + m * 128:hh * 512 + (m + 1) * 128],
                                 b=pb[hh][:, m * 128:(m + 1) * 128],
                                 s=u4a[:, u4base + m:u4base + m + 1]:
                                 se.activation(o, b, AF.Copy, scale=s),
                                 inc=SC if (hh == 3 and m == 3) else None)
                    sc_grp[k] = SC.n
                    P.wait("gpsimd", SC, sc_grp[k])
                    if k == 0:
                        P.wait("gpsimd", EV, ev_E)
                        P.wait("gpsimd", ST, last_spill[cp])
                    for hh in range(4):
                        h = k * 4 + hh
                        P.on("gpsimd", lambda gp, o=mtmp[:, hh * 256:hh * 256 + 128],
                             a=locb[:, h * 128:(h + 1) * 128],
                             b=sh1[:, hh * 512:hh * 512 + 128]:
                             gp.tensor_add(o, a, b))
                    for hh in range(4):
                        P.on("gpsimd", lambda gp,
                             o=mtmp[:, hh * 256 + 128:hh * 256 + 256],
                             a=sh1[:, hh * 512 + 128:hh * 512 + 256],
                             b=sh1[:, hh * 512 + 256:hh * 512 + 384]:
                             gp.tensor_add(o, a, b),
                             inc=CH["gpsimd"] if hh == 3 else None)
                    cgb = CH["gpsimd"].n
                    P.wait("gpsimd", CH["gpsimd"], cgb)
                    for hh in range(4):
                        P.on("gpsimd", lambda gp, o=mtmp[:, hh * 256:hh * 256 + 128],
                             b=mtmp[:, hh * 256 + 128:hh * 256 + 256]:
                             gp.tensor_add(o, o, b),
                             inc=CH["gpsimd"] if hh == 3 else None)
                    cgc = CH["gpsimd"].n
                    P.wait("gpsimd", CH["gpsimd"], cgc)
                    for hh in range(4):
                        h = k * 4 + hh
                        P.on("gpsimd", lambda gp,
                             o=otK[:, cp * H + h * 128:cp * H + (h + 1) * 128],
                             a=mtmp[:, hh * 256:hh * 256 + 128],
                             b=sh1[:, hh * 512 + 384:hh * 512 + 512]:
                             gp.tensor_add(o, a, b), inc=GP if hh == 3 else None)
                    gp_grp[k] = GP.n
                    gp_lane_free = GP.n
                gp_read_done = GP.n
                sc_pb03 = sc_grp[3]
                u4free[cp] = sc_grp[3]
                P.drain("gpsimd", GP)
                P.wait("sync", GP)
                P.dma("sync", ST, out_pre[t0 + c * 128:t0 + (c + 1) * 128, :],
                      otK[:, cp * H:(cp + 1) * H])
                last_spill[cp] = ST.n
            mm_g_end[g] = MM.n
        P.reset(base)
        P.barrier(BAR)
        dump("outT", out_pre[:, :])

    if "B3w" in active:
        P.barrier(BAR)
        P.reset(S_BASE)
        base = P.mark()
        wo_s = P.salloc("bw_wo", (128, 16 * H), BF16)
        oT_s = P.salloc("bw_oT", (128, 16 * GT), BF16)
        x_s = P.salloc("bw_x", (128, 2 * H), F32)
        y_s = P.salloc("bw_y", (128, 2 * H), F32)
        P.wait("sync", ST)
        for ct in range(16):
            P.dma("gpsimd", LDW, wo_s[:, ct * H:(ct + 1) * H], woT[ct * 128:(ct + 1) * 128, :])
        P.wait("tensor", LDW)
        ev_sn3 = []
        st_sn3 = []
        for g in range(NG):
            t0 = g * GT
            P.wait("sync", MM)
            for ct in range(16):
                P.dma_tr(LD, oT_s[:, ct * GT:(ct + 1) * GT],
                         out_pre[t0:t0 + GT, ct * 128:(ct + 1) * 128])
            for ts_ in range(4):
                xb = x_s[:, (ts_ % 2) * H:(ts_ % 2 + 1) * H]
                yb = y_s[:, (ts_ % 2) * H:(ts_ % 2 + 1) * H]
                if len(st_sn3) >= 2:
                    P.wait("sync", ST, st_sn3[-2])
                P.dma("sync", LD, xb, x_own[t0 + ts_ * 128:t0 + (ts_ + 1) * 128, :])
                P.wait("tensor", LD)
                for oc in range(4):
                    bank = pb[oc % 2]
                    if len(ev_sn3) >= 2:
                        P.wait("tensor", EV, ev_sn3[-2])
                    for ct in range(16):
                        P.on("tensor", lambda te, o=bank[:, :],
                             l=oT_s[:, ct * GT + ts_ * 128:ct * GT + (ts_ + 1) * 128],
                             r=wo_s[:, ct * H + oc * 512:ct * H + (oc + 1) * 512],
                             st=(ct == 0), sp=(ct == 15):
                             te.matmul(o, l, r, start=st, stop=sp),
                             inc=MM if ct == 15 else None)
                    P.wait("vector", MM)
                    P.on("vector", lambda ve, b=bank, xb=xb, yb=yb, oc=oc: ve.tensor_add(
                        yb[:, oc * 512:(oc + 1) * 512], b[:, :],
                        xb[:, oc * 512:(oc + 1) * 512]), inc=EV)
                    ev_sn3.append(EV.n)
                P.drain("vector", EV)
                P.wait("sync", EV)
                P.dma("sync", ST, out[t0 + ts_ * 128:t0 + (ts_ + 1) * 128, :], yb)
                st_sn3.append(ST.n)
        P.reset(base)

    def ffn_pass(tag, wg_src, wu_src, wd_src, cw_idx, ov=None):
        if ov is None:
            P.barrier(BAR)
        P.reset(S_BASE)
        base = P.mark()
        wg_s = P.salloc(f"{tag}_wg", (128, 16 * F), BF16)
        wu_s = P.salloc(f"{tag}_wu", (128, 16 * F), BF16)
        wd_s = P.salloc(f"{tag}_wd", (128, NFT * H), BF16)
        hiT_s = P.salloc(f"{tag}_hiT", (128, 2 * 16 * GT), BF16)
        g_bf = P.salloc(f"{tag}_g", (128, NFT * GT), BF16)
        s1_s = P.salloc(f"{tag}_s1", (128, 2 * GT), BF16)
        yd_s = P.salloc(f"{tag}_yd", (128, 2 * H), F32)

        if ov is None:
            P.wait("gpsimd", ST)
            P.wait("sync", ST)
        else:
            P.wait("gpsimd", MM, ov["mm_up"])
        for ct in range(16):
            P.dma("gpsimd", LDW, wg_s[:, ct * F:(ct + 1) * F], wg_src[ct * 128:(ct + 1) * 128, :])
            P.dma("gpsimd", LDW, wu_s[:, ct * F:(ct + 1) * F], wu_src[ct * 128:(ct + 1) * 128, :])
        if ov is not None:
            P.wait("gpsimd", MM2, ov["mm2"])
        for ft in range(NFT):
            P.dma("gpsimd", LDW, wd_s[:, ft * H:(ft + 1) * H], wd_src[ft * 128:(ft + 1) * 128, :])
        if ov is not None:
            P.wait("tensor", EV2, ov["ev2"])
        P.wait("tensor", LDW)

        ld_sn = {}
        ev_g = []
        ev_y = []
        st_y = []
        for g in range(NG):
            t0 = g * GT
            hbuf = hiT_s[:, (g % 2) * 16 * GT:(g % 2 + 1) * 16 * GT]
            if g >= 2:
                P.wait("sync", MM, ld_sn[g - 2])
            elif ov is not None:
                P.wait("sync", MM, ov["ld2"][g])
            P.dma("sync", LD, hbuf.rearrange("p (ct t) -> p ct t", t=GT),
                  hiT_cm[:, t0:t0 + GT].rearrange("(ct p) t -> p ct t", p=128))
            P.wait("tensor", LD)
            P.wait("tensor", MM2)
            for ft in range(NFT):
                b1 = pb[ft % 2]
                b2 = pb[2 + ft % 2]
                if len(ev_g) >= 2:
                    P.wait("tensor", EV, ev_g[-2])
                for ct in range(16):
                    P.on("tensor", lambda te, o=b1[:, :],
                         l=wg_s[:, ct * F + ft * 128:ct * F + (ft + 1) * 128],
                         r=hbuf[:, ct * GT:(ct + 1) * GT],
                         st=(ct == 0), sp=(ct == 15): te.matmul(o, l, r, start=st, stop=sp))
                for ct in range(16):
                    P.on("tensor", lambda te, o=b2[:, :],
                         l=wu_s[:, ct * F + ft * 128:ct * F + (ft + 1) * 128],
                         r=hbuf[:, ct * GT:(ct + 1) * GT],
                         st=(ct == 0), sp=(ct == 15): te.matmul(o, l, r, start=st, stop=sp),
                         inc=MM if ct == 15 else None)
                P.wait("scalar", MM)
                P.on("scalar", lambda se, b=b1, ft=ft: se.activation(
                    s1_s[:, (ft % 2) * GT:(ft % 2 + 1) * GT], b[:, :], AF.Silu), inc=SC)
                P.wait("vector", SC)
                P.on("vector", lambda ve, b=b2, ft=ft: ve.tensor_mul(
                    g_bf[:, ft * GT:(ft + 1) * GT],
                    s1_s[:, (ft % 2) * GT:(ft % 2 + 1) * GT], b[:, :]), inc=EV)
                ev_g.append(EV.n)
            P.drain("vector", EV)
            P.wait("tensor", EV)
            for ts_ in range(4):
                yb = yd_s[:, (ts_ % 2) * H:(ts_ % 2 + 1) * H]
                if len(st_y) >= 2:
                    P.wait("vector", ST, st_y[-2])
                    P.wait("scalar", ST, st_y[-2])
                elif ov is not None:
                    P.wait("vector", ST, ov["st"])
                    P.wait("scalar", ST, ov["st"])
                for oc in range(4):
                    bank = pb[4 + oc % 2]
                    if len(ev_y) >= 2:
                        P.wait("tensor", EV2, ev_y[-2])
                    for ft in range(NFT):
                        P.on("tensor", lambda te, o=bank[:, :],
                             l=g_bf[:, ft * GT + ts_ * 128:ft * GT + (ts_ + 1) * 128],
                             r=wd_s[:, ft * H + oc * 512:ft * H + (oc + 1) * 512],
                             st=(ft == 0), sp=(ft == NFT - 1):
                             te.matmul(o, l, r, start=st, stop=sp),
                             inc=MM2 if ft == NFT - 1 else None)
                    ysl = yb[:, oc * 512:(oc + 1) * 512]
                    if cw_idx is None:
                        if oc % 2 == 0:
                            P.wait("vector", MM2)
                            P.on("vector", lambda ve, b=bank, o=ysl: ve.tensor_copy(o, b[:, :]),
                                 inc=EV2)
                        else:
                            P.wait("scalar", MM2)
                            P.on("scalar", lambda se, b=bank, o=ysl: se.copy(o, b[:, :]),
                                 inc=EV2)
                    else:
                        cwc = cw_all[:, (g * 4 + ts_) * 8 + cw_idx:(g * 4 + ts_) * 8 + cw_idx + 1]
                        if oc % 2 == 0:
                            P.wait("vector", MM2)
                            P.on("vector", lambda ve, b=bank, o=ysl, cw=cwc: ve.tensor_scalar(
                                o, b[:, :], cw, None, op0=ALU.mult), inc=EV2)
                        else:
                            P.wait("scalar", MM2)
                            P.on("scalar", lambda se, b=bank, o=ysl, cw=cwc: se.activation(
                                o, b[:, :], AF.Copy, scale=cw), inc=EV2)
                    ev_y.append(EV2.n)
                P.drain("vector", EV2)
                P.drain("scalar", EV2)
                P.wait("gpsimd", EV2)
                if ov is not None and g == 0 and ts_ == 0:
                    P.wait("gpsimd", ST, ov["st"])
                P.dma("gpsimd", ST, out[t0 + ts_ * 128:t0 + (ts_ + 1) * 128, :], yb,
                      accum_op=ALU.add)
                st_y.append(ST.n)
            ld_sn[g] = MM.n
        P.reset(base)
        return {"mm_up": MM.n, "mm2": MM2.n, "ev2": EV2.n, "st": ST.n,
                "ld2": [ld_sn[NG - 2], ld_sn[NG - 1]]}

    if "S0" in active:
        ffn_pass("s0", swg[:, 0:F], swu[:, 0:F], swd[0:F, :], None)
    if "S1" in active:
        ffn_pass("s1", swg[:, F:FS], swu[:, F:FS], swd[F:FS, :], None)
    for e in range(E):
        if f"E{e}" in active:
            ffn_pass(f"e{e}", mwg[e], mwu[e], mwd[e], e)

    P.barrier(BAR)
    P.wait("gpsimd", ST)
    P.wait("sync", ST)

    P.emit()
    return nc, dbg


def _prep_inputs(inputs):
    f32 = np.float32
    bf = ml_dtypes.bfloat16
    x = np.asarray(inputs["L0_attn_hidden_states"], f32)
    tp = lambda a: np.ascontiguousarray(np.asarray(a, f32).T)
    com = {
        "wqT": tp(inputs["L0_attn_w_q"]),
        "wkT": tp(inputs["L0_attn_w_k"]),
        "wvT": tp(inputs["L0_attn_w_v"]),
        "woT": tp(inputs["L0_attn_w_o"]),
        "eg1T": tp(inputs["L0_attn_eg_w1"]),
        "eg1b": np.ascontiguousarray(np.asarray(inputs["L0_attn_eg_b1"], f32)),
        "eg2T": np.ascontiguousarray(np.asarray(inputs["L0_attn_eg_w2"], f32)[0]),
        "eg2b": np.asarray(inputs["L0_attn_eg_b2"], f32).reshape(1, 1),
        "mgate": np.asarray(inputs["L0_attn_memory_gate"], f32).reshape(1, NH),
        "memcat": np.ascontiguousarray(
            np.asarray(inputs["L0_attn_fine_memories"], f32).transpose(2, 1, 0, 3)
            .reshape(HD, NH * 4 * HD)),
        "normcat": np.ascontiguousarray(
            np.asarray(inputs["L0_attn_fine_memory_norms"], f32).transpose(2, 1, 0)
            .reshape(HD, NH * 4)),
        "gwT": tp(inputs["L0_ffn_gate_weight"]),
        "mwg": np.ascontiguousarray(np.asarray(inputs["L0_ffn_w_gate"], f32).transpose(0, 2, 1)),
        "mwu": np.ascontiguousarray(np.asarray(inputs["L0_ffn_w_up"], f32).transpose(0, 2, 1)),
        "mwd": np.ascontiguousarray(np.asarray(inputs["L0_ffn_w_down"], f32).transpose(0, 2, 1)),
        "swg": tp(inputs["L0_ffn_sw_gate"]),
        "swu": tp(inputs["L0_ffn_sw_up"]),
        "swd": tp(inputs["L0_ffn_sw_down"]),
        "c_id_f": np.eye(128, dtype=f32),
        "c_triu_f": np.triu(np.ones((128, 128), dtype=f32)),
        "c_ones_f": np.ones((128, 130), dtype=f32),
    }
    in_maps = []
    for c in range(8):
        b, half = c // 2, c % 2
        m = dict(com)
        m["x_own"] = np.ascontiguousarray(x[b, half * T:(half + 1) * T])
        m["x_oth"] = np.ascontiguousarray(x[b, (1 - half) * T:(2 - half) * T])
        pm = np.empty((128, 2), f32)
        pm[:, 0] = 1.0 if half == 1 else 0.0
        pm[:, 1] = 1e-5
        m["c_premask"] = pm
        in_maps.append(m)
    return in_maps


def kernel(**inputs):
    nc, _ = build()
    in_maps = _prep_inputs(inputs)
    res = run_bass_kernel_spmd(nc, in_maps, core_ids=list(range(8)))
    outp = np.empty((B_, S_, H), np.float32)
    for c in range(8):
        b, half = c // 2, c % 2
        outp[b, half * T:(half + 1) * T] = res.results[c]["out"]
    return outp


# version 07
# speedup vs baseline: 1.0950x; 1.0950x over previous
import numpy as np
import ml_dtypes
import concourse.bass as bass
import concourse.mybir as mybir
from concourse.bass_utils import run_bass_kernel_spmd

F32 = mybir.dt.float32
BF16 = mybir.dt.bfloat16
AF = mybir.ActivationFunctionType
ALU = mybir.AluOpType
AX = mybir.AxisListType

B_, S_, H = 4, 8192, 2048
NH, HD = 16, 128
T = 4096
GT = 512
NG = 8
E, F, FS = 8, 1408, 2816
NFT = 11
EPS = 1e-6
SBUF_BASE = 16384 + 1024
SBUF_BYTES = SBUF_BASE + 205 * 1024

PASSES = ["A1", "A2", "B1", "B2", "P", "B3a", "B3w", "S0", "S1"] + [f"E{i}" for i in range(E)]


class Sem:
    def __init__(self, nc, name):
        self._cm = nc.semaphore(name)
        self.h = self._cm.__enter__()
        self.n = 0


class Prog:
    ENGS = ("tensor", "vector", "scalar", "sync", "gpsimd")

    def __init__(self, nc):
        self.nc = nc
        self.ops = {e: [] for e in self.ENGS}
        self.sbuf_off = SBUF_BASE

    def salloc(self, name, shape, dtype, align=64):
        esz = 2 if dtype == BF16 else 4
        self.sbuf_off = (self.sbuf_off + align - 1) // align * align
        t = self.nc.alloc_sbuf_tensor_at(name, list(shape), dtype, offset=self.sbuf_off)
        self.sbuf_off += int(np.prod(shape[1:])) * esz
        assert self.sbuf_off <= SBUF_BYTES, f"SBUF overflow at {name}: {self.sbuf_off}"
        return t[tuple(slice(None) for _ in shape)]

    def mark(self):
        return self.sbuf_off

    def reset(self, off):
        self.sbuf_off = off

    def on(self, eng, fn, inc=None, k=1):
        incs = []
        if inc is not None:
            incs = inc if isinstance(inc, list) else [(inc, k)]
        if incs:
            def g(EE, f=fn, ii=[(s.h, kk) for s, kk in incs]):
                r = f(EE)
                r.then_inc(*ii[0])
                for h, kk in ii[1:]:
                    EE.nop().then_inc(h, kk)
            self.ops[eng].append(g)
            for s, kk in incs:
                s.n += kk
        else:
            self.ops[eng].append(fn)

    def wait(self, eng, sem, val=None):
        v = sem.n if val is None else val
        if v > 0:
            self.ops[eng].append(lambda EE, s=sem.h, vv=v: EE.wait_ge(s, vv))

    def dma(self, eng, sem, dst, src, **kw):
        def f(EE, d=dst, s=src, i=sem.h, kw=kw):
            EE.dma_start(out=d, in_=s, **kw).then_inc(i, 16)
        self.ops[eng].append(f)
        sem.n += 16

    def dma_tr(self, sem, dst, src):
        def f(EE, d=dst, s=src, i=sem.h):
            EE.dma_start_transpose(d, s).then_inc(i, 16)
        self.ops["sync"].append(f)
        sem.n += 16

    def drain(self, eng, sem, k=1):
        self.ops[eng].append(lambda EE, s=sem.h, kk=k: EE.drain().then_inc(s, kk))
        sem.n += k

    def barrier(self, bs):
        for e in self.ENGS:
            if e in ("vector", "scalar", "gpsimd"):
                self.ops[e].append(lambda EE, s=bs.h: EE.drain().then_inc(s, 1))
            else:
                self.ops[e].append(lambda EE, s=bs.h: EE.nop().then_inc(s, 1))
        bs.n += len(self.ENGS)
        for e in self.ENGS:
            self.ops[e].append(lambda EE, s=bs.h, vv=bs.n: EE.wait_ge(s, vv))

    def emit(self):
        with self.nc.Block() as block:
            @block.tensor
            def _(te):
                for f in self.ops["tensor"]:
                    f(te)

            @block.vector
            def _(ve):
                for f in self.ops["vector"]:
                    f(ve)

            @block.scalar
            def _(se):
                for f in self.ops["scalar"]:
                    f(se)

            @block.sync
            def _(sy):
                for f in self.ops["sync"]:
                    f(sy)

            @block.gpsimd
            def _(gp):
                for f in self.ops["gpsimd"]:
                    f(gp)


def build(upto="ALL", debug=(), b3phases="aceb", b3ng=NG):
    stop_at = len(PASSES) if upto == "ALL" else PASSES.index(upto) + 1
    active = set(PASSES[:stop_at])

    nc = bass.Bass("TRN2", target_bir_lowering=False, debug=False)
    P = Prog(nc)

    din = lambda n, shp, dt=F32: nc.dram_tensor(n, list(shp), dt, kind="ExternalInput").ap()
    x_own = din("x_own", (T, H))
    x_oth = din("x_oth", (T, H))
    wqT = din("wqT", (H, H))
    wkT = din("wkT", (H, H))
    wvT = din("wvT", (H, H))
    woT = din("woT", (H, H))
    eg1T = din("eg1T", (H, 512))
    eg1b = din("eg1b", (512,))
    eg2T = din("eg2T", (512,))
    eg2b = din("eg2b", (1, 1))
    mgate = din("mgate", (1, NH))
    memcat = din("memcat", (HD, NH * 4 * HD))
    normcat = din("normcat", (HD, NH * 4))
    gwT = din("gwT", (H, E))
    mwg = din("mwg", (E, H, F))
    mwu = din("mwu", (E, H, F))
    mwd = din("mwd", (E, F, H))
    swg = din("swg", (H, FS))
    swu = din("swu", (H, FS))
    swd = din("swd", (FS, H))
    c_id_f = din("c_id_f", (128, 128))
    c_triu_f = din("c_triu_f", (128, 128))
    c_ones_f = din("c_ones_f", (128, 130))
    c_premask = din("c_premask", (128, 2))

    out = nc.dram_tensor("out", [T, H], F32, kind="ExternalOutput").ap()

    dscr = lambda n, shp, dt: nc.dram_tensor(n, list(shp), dt, kind="Internal").ap()
    hi_tok = dscr("hi_tok", (2 * T, H), BF16)
    hiT_cm = dscr("hiT_cm", (H, 2 * T), BF16)
    lo_tok = dscr("lo_tok", (T, H), BF16)
    sqT_d = dscr("sqT_d", (H, T), BF16)
    skT_d = dscr("skT_d", (H, 2 * T), BF16)
    sk_tok = dscr("sk_tok", (2 * T, H), BF16)
    v_tok = dscr("v_tok", (2 * T, H), BF16)
    out_pre = dscr("out_pre", (T, H), BF16)
    relscr = dscr("relscr", (4, 16), F32)

    DBG_SPECS = {
        "hi": ((2 * T, H), BF16), "lo": ((T, H), BF16), "sqT": ((H, T), BF16),
        "skT": ((H, 2 * T), BF16), "vtk": ((2 * T, H), BF16), "hiT": ((H, 2 * T), BF16),
        "cw": ((128, 256), F32), "wrow": ((1, 160), F32), "S0": ((128, NH * 128), F32),
        "outT": ((T, H), BF16), "exr": ((1, T), F32), "a1c": ((128, 96), F32),
        "relo": ((4, 16), F32), "memT": ((T, H), BF16), "locT": ((T, H), BF16),
        "gtl": ((128, NH * 128), BF16), "mixT": ((T, H), BF16),
    }
    dbg = {}
    for name in debug:
        shp, dt = DBG_SPECS[name]
        dbg[name] = nc.dram_tensor("dbg_" + name, list(shp), dt, kind="ExternalOutput").ap()
    DBG_MLO = ("memT" in dbg)

    sems = {k: Sem(nc, k) for k in
            ("BAR", "LD", "LDW", "MM", "MM2", "EV", "EV2", "EV3", "ST", "SC", "GP",
             "CV", "CS", "CG")}
    BAR, LD, LDW, MM, MM2, EV, EV2, EV3, ST, SC, GP = (sems[k] for k in
            ("BAR", "LD", "LDW", "MM", "MM2", "EV", "EV2", "EV3", "ST", "SC", "GP"))
    CH = {"vector": sems["CV"], "scalar": sems["CS"], "gpsimd": sems["CG"]}

    def ch(eng, fn, also=None, k=1):
        P.wait(eng, CH[eng])
        inc = [(CH[eng], 1)] + ([(also, k)] if also is not None else [])
        P.on(eng, fn, inc=inc)

    pb = [nc.alloc_psum_tensor(f"pb{i}", [128, 512], F32) for i in range(8)]

    cst = P.salloc("cst", (128, 640), BF16)
    id_bf = cst[:, 0:128]
    triu = cst[:, 128:256]
    ones_col = cst[:, 256:257]
    ones_row = cst[0:1, 258:386]
    wrow_b = cst[0:1, 448:513]
    grow_b = cst[0:1, 520:536]
    cstf = P.salloc("cstf", (128, 160), F32)
    id_f = cstf[:, 0:128]
    premask = cstf[:, 128:129]
    eps_col = cstf[:, 129:130]
    mg_row = cstf[0:1, 130:146]
    cw_all = P.salloc("cw_all", (128, 256), F32)
    wko = P.salloc("wko", (128, 96), F32)
    wcols = wko[:, 0:64]
    zf_col = wko[:, 64:65]
    gcols_f = wko[:, 65:81]
    rows = P.salloc("rows", (128, 256), F32)
    rel_acc = P.salloc("rel_acc", (4, 16), F32)
    S_BASE = P.mark()
    S_fS = P.salloc("S_fS", (128, NH * 128), F32)
    S_fz = P.salloc("S_fz", (128, NH), F32)
    S_bS = P.salloc("S_bS", (128, NH * 128), BF16)
    S_bz = P.salloc("S_bz", (128, NH), BF16)
    PERSIST = P.mark()

    P.dma("sync", LD, cstf[:, 0:128], c_id_f[:, :])
    P.dma("sync", LD, cstf[:, 128:130], c_premask[:, :])
    P.dma("sync", LD, cstf[0:1, 130:146], mgate[:, :])
    P.dma("sync", LD, rows[:, 0:128], c_triu_f[:, :])
    P.wait("vector", LD)
    P.on("vector", lambda ve: ve.tensor_copy(cst[:, 0:128], cstf[:, 0:128]))
    P.on("vector", lambda ve: ve.tensor_copy(cst[:, 128:256], rows[:, 0:128]), inc=EV)
    P.wait("sync", EV)
    P.dma("sync", LD, rows[:, 0:130], c_ones_f[:, :])
    P.wait("vector", LD)
    P.on("vector", lambda ve: ve.tensor_copy(cst[:, 256:258], rows[:, 0:2]))
    P.on("vector", lambda ve: ve.tensor_copy(cst[0:1, 258:386], rows[0:1, 2:130]), inc=EV)
    CONST_LD = LD.n

    def dump(name, src):
        if name in dbg:
            P.barrier(BAR)
            P.wait("sync", ST)
            P.dma("sync", ST, dbg[name][:, :], src)

    if "A1" in active:
        base = P.mark()
        xch = P.salloc("a1_x", (128, 2 * H), F32)
        xnf = P.salloc("a1_xn", (128, H), F32)
        hi_st = P.salloc("a1_hi", (128, 2 * H), BF16)
        lo_st = P.salloc("a1_lo", (128, H), BF16)
        colz = P.salloc("a1_c", (128, 96), F32)

        ev_sn, gp_sn, st_sn = [], [], []
        for i in range(64):
            own = i < 32
            src = x_own if own else x_oth
            r0 = (i % 32) * 128
            buf = xch[:, (i % 2) * H:(i % 2 + 1) * H]
            hbuf = hi_st[:, (i % 2) * H:(i % 2 + 1) * H]
            c = colz[:, (i % 2) * 48:(i % 2) * 48 + 48]
            if i >= 2:
                P.wait("sync", EV, ev_sn[i - 2])
            P.dma("sync", LD, buf, src[r0:r0 + 128, :])
            P.wait("vector", LD)
            P.wait("vector", SC)
            for j in range(4):
                P.on("vector", lambda ve, b=buf, c=c, j=j: ve.bn_stats(
                    c[:, j * 6:(j + 1) * 6], b[:, j * 512:(j + 1) * 512]),
                     inc=CH["vector"] if j == 3 else None)
            mv = c[:, 24:32].rearrange("p (a b) -> p a b", b=2)
            ms = mv[:, :, 0:1].squeeze(2)
            vs = mv[:, :, 1:2].squeeze(2)
            P.wait("vector", CH["vector"])
            for j in range(4):
                P.on("vector", lambda ve, c=c, j=j: ve.bn_aggr(
                    c[:, 24 + 2 * j:26 + 2 * j], c[:, j * 6:(j + 1) * 6]),
                     inc=CH["vector"] if j == 3 else None)
            ch("vector", lambda ve, ms=ms, c=c: ve.reduce_sum(c[:, 32:33], ms, axis=AX.X))
            P.on("vector", lambda ve, vs=vs, c=c: ve.reduce_sum(c[:, 33:34], vs, axis=AX.X))
            P.on("vector", lambda ve, ms=ms, c=c: ve.tensor_mul(c[:, 36:40], ms, ms),
                 inc=CH["vector"])
            ch("vector", lambda ve, c=c: ve.reduce_sum(c[:, 34:35], c[:, 36:40], axis=AX.X))
            ch("vector", lambda ve, c=c: ve.tensor_scalar(
                c[:, 24:25], c[:, 32:33], 0.25, None, op0=ALU.mult))
            ch("vector", lambda ve, c=c: ve.tensor_mul(c[:, 41:42], c[:, 24:25], c[:, 24:25]))
            ch("vector", lambda ve, c=c: ve.tensor_scalar(
                c[:, 42:43], c[:, 33:34], 0.25, None, op0=ALU.mult))
            ch("vector", lambda ve, c=c: ve.tensor_scalar(
                c[:, 43:44], c[:, 34:35], 0.25, None, op0=ALU.mult))
            ch("vector", lambda ve, c=c: ve.tensor_add(c[:, 44:45], c[:, 42:43], c[:, 43:44]))
            ch("vector", lambda ve, c=c: ve.tensor_sub(c[:, 25:26], c[:, 44:45], c[:, 41:42]),
               also=EV2)
            P.wait("scalar", EV2)
            ch("scalar", lambda se, c=c: se.activation(c[:, 26:27], c[:, 25:26], AF.Ln,
                                                       bias=eps_col))
            ch("scalar", lambda se, c=c: se.activation(c[:, 27:28], c[:, 26:27], AF.Exp,
                                                       scale=-0.5), also=SC)
            P.wait("vector", SC)
            ch("vector", lambda ve, c=c: ve.tensor_scalar(
                c[:, 28:29], c[:, 24:25], c[:, 27:28], -1.0, op0=ALU.mult, op1=ALU.mult))
            if own:
                P.wait("vector", GP)
            if i >= 2:
                P.wait("vector", ST, st_sn[i - 2])
            ch("vector", lambda ve, b=buf, hb=hbuf, c=c: ve.tensor_scalar(
                hb, b, c[:, 27:28], c[:, 28:29], op0=ALU.mult, op1=ALU.add), also=EV)
            P.drain("vector", EV)
            if own:
                P.on("vector", lambda ve, b=buf, c=c: ve.tensor_scalar(
                    xnf, b, c[:, 27:28], c[:, 28:29], op0=ALU.mult, op1=ALU.add), inc=EV)
                P.wait("gpsimd", EV)
                P.wait("gpsimd", ST)
                P.on("gpsimd", lambda gp, hb=hbuf: gp.tensor_sub(lo_st[:, :], xnf, hb))
                P.drain("gpsimd", GP)
                P.wait("sync", GP)
                P.dma("sync", ST, lo_tok[r0:r0 + 128, :], lo_st[:, :])
            P.wait("sync", EV)
            dr = r0 if own else T + r0
            P.dma("sync", ST, hi_tok[dr:dr + 128, :], hbuf)
            ev_sn.append(EV.n)
            gp_sn.append(GP.n)
            st_sn.append(ST.n)
        P.barrier(BAR)
        if "a1c" in dbg:
            P.dma("sync", ST, dbg["a1c"][:, :], colz)
        P.reset(base)
        P.barrier(BAR)
        dump("hi", hi_tok[:, :])
        dump("lo", lo_tok[:, :])

    def _nf_rel(h, sq_s, nrm_s, ev3_sn):
        bank = pb[2 + (h % 2)]
        P.wait("tensor", EV)
        if h >= 2:
            P.wait("tensor", EV3, ev3_sn[h - 2])
        P.on("tensor", lambda te, o=bank[0:4, :], l=nrm_s[:, h * 4:(h + 1) * 4],
             r=sq_s[:, h * GT:(h + 1) * GT]: te.matmul(o, l, r, start=True, stop=True),
             inc=MM2)
        P.wait("vector", MM2)
        ch("vector", lambda ve, b=bank: ve.reduce_sum(rows[0:4, 254:255], b[0:4, :],
                                                      axis=AX.X))
        ch("vector", lambda ve, h=h: ve.tensor_add(rel_acc[:, h:h + 1],
                                                   rel_acc[:, h:h + 1],
                                                   rows[0:4, 254:255]), also=EV3)
        ev3_sn.append(EV3.n)

    def proj_pass(kind):
        base = P.mark()
        w_s = P.salloc(f"{kind}_w", (128, 16 * H), BF16)
        hiT_s = P.salloc(f"{kind}_hiT", (128, 2 * 16 * GT), BF16)
        sq_s = P.salloc(f"{kind}_sq", (128, 16 * GT), BF16)
        ew = P.salloc(f"{kind}_ew", (128, 6 * GT), F32)
        if kind == "k":
            tokst = P.salloc("k_tokst", (128, 2 * 8192), BF16)
            st_tok = {}
            mm_tr = {}
        if kind == "q":
            nrm_s = P.salloc("a2_nrm", (128, NH * 4), BF16)
            gw_s = P.salloc("a2_gw", (128, 256), BF16)
            gw_f = P.salloc("a2_gwf", (128, 128), F32)
            loT_s = P.salloc("a2_loT", (128, 16 * GT), BF16)
            lgt = P.salloc("a2_lg", (128, 1024), F32)
            lgrow = lgt[0:8, 0:512]
            lgtok = lgt[:, 512:544]
            tk = lgt[:, 544:576]

        wsrc = wqT if kind == "q" else wkT
        P.wait("sync", ST)
        for ct in range(16):
            P.dma("gpsimd", LDW, w_s[:, ct * H:(ct + 1) * H], wsrc[ct * 128:(ct + 1) * 128, :])
        if kind == "q":
            P.dma("gpsimd", LDW, nrm_s[:, :], normcat[:, :])
            P.dma("gpsimd", LDW, gw_f.rearrange("p (a e) -> p a e", e=8),
                  gwT.rearrange("(a p) e -> p a e", p=128))
            P.wait("vector", LDW)
            ch("vector", lambda ve: ve.tensor_copy(gw_s[:, 0:128], gw_f[:, :]))
            ch("vector", lambda ve: ve.tensor_sub(gw_f[:, :], gw_f[:, :], gw_s[:, 0:128]))
            ch("vector", lambda ve: ve.tensor_copy(gw_s[:, 128:256], gw_f[:, :]))
            P.drain("vector", EV3)
            P.on("vector", lambda ve: ve.memset(rel_acc[:, :], 0.0), inc=EV3)
        P.wait("tensor", LDW)

        mm_sn = {}
        ev_sn = []
        st_sn = {}
        for g in range(16):
            own = g < 8
            t0 = (g * GT) if own else (T + (g - 8) * GT)
            hbuf = hiT_s[:, (g % 2) * 16 * GT:(g % 2 + 1) * 16 * GT]
            if g >= 2:
                P.wait("sync", MM, mm_sn[g - 2])
                if kind == "q":
                    P.wait("sync", ST, st_sn[g - 2])
            if kind == "q":
                for ct in range(16):
                    P.dma_tr(LD, hbuf[:, ct * GT:(ct + 1) * GT],
                             hi_tok[t0:t0 + GT, ct * 128:(ct + 1) * 128])
            else:
                P.dma("sync", LD, hbuf.rearrange("p (ct t) -> p ct t", t=GT),
                      hiT_cm[:, t0:t0 + GT].rearrange("(ct p) t -> p ct t", p=128))
            if kind == "q":
                P.wait("sync", LD)
                P.dma("sync", ST, hiT_cm[:, t0:t0 + GT].rearrange(
                    "(ct p) t -> p ct t", p=128), hbuf.rearrange("p (ct t) -> p ct t", t=GT))
                st_sn[g] = ST.n
            if kind == "q" and own:
                P.wait("sync", MM)
                for ct in range(16):
                    P.dma_tr(LD, loT_s[:, ct * GT:(ct + 1) * GT],
                             lo_tok[g * GT:g * GT + GT, ct * 128:(ct + 1) * 128])
            P.wait("tensor", LD)
            if kind == "q" and g == 0:
                P.wait("tensor", EV3)

            if kind == "q" and own:
                pairs = [(gw_s[:, 0:128], hbuf), (gw_s[:, 128:256], hbuf),
                         (gw_s[:, 0:128], loT_s)]
                nmm = 0
                for pi, (gsl, xs) in enumerate(pairs):
                    for ct in range(16):
                        last = (pi == 2 and ct == 15)
                        P.on("tensor", lambda te, o=pb[4][0:8, :],
                             l=gsl[:, ct * 8:(ct + 1) * 8],
                             r=xs[:, ct * GT:(ct + 1) * GT],
                             st=(nmm == 0), sp=last: te.matmul(o, l, r, start=st, stop=sp),
                             inc=MM if last else None)
                        nmm += 1
                P.wait("vector", MM)
                P.on("vector", lambda ve: ve.tensor_copy(lgrow, pb[4][0:8, :]))
                P.drain("vector", EV2)
                P.wait("tensor", EV2)
                for sb in range(4):
                    P.on("tensor", lambda te, sb=sb: te.transpose(
                        pb[5][:, sb * 8:sb * 8 + 8], lgrow[0:8, sb * 128:(sb + 1) * 128],
                        id_f[0:8, 0:8]), inc=MM if sb == 3 else None)
                P.wait("vector", MM)
                ch("vector", lambda ve: ve.tensor_copy(lgtok, pb[5][:, 0:32]), also=EV2)
                for sb in range(4):
                    lgq = lgtok[:, sb * 8:(sb + 1) * 8]
                    cwo = cw_all[:, (g * 4 + sb) * 8:(g * 4 + sb) * 8 + 8]
                    ch("vector", lambda ve, q=lgq: ve.max(out=tk[:, 0:8], in_=q))
                    ch("vector", lambda ve: ve.tensor_scalar(
                        tk[:, 8:9], tk[:, 0:1], -1.0, None, op0=ALU.mult))
                    ch("vector", lambda ve, q=lgq: ve.tensor_scalar(
                        tk[:, 16:24], q, tk[:, 1:2], None, op0=ALU.is_ge), also=EV2)
                    P.wait("scalar", EV2)
                    P.on("scalar", lambda se, q=lgq: se.activation(
                        tk[:, 24:32], q, AF.Exp, bias=tk[:, 8:9]), inc=SC)
                    P.wait("vector", SC)
                    ch("vector", lambda ve: ve.tensor_mul(tk[:, 24:32], tk[:, 24:32],
                                                          tk[:, 16:24]))
                    ch("vector", lambda ve: ve.reduce_sum(tk[:, 9:10], tk[:, 24:32],
                                                          axis=AX.X))
                    ch("vector", lambda ve: ve.reciprocal(tk[:, 10:11], tk[:, 9:10]))
                    ch("vector", lambda ve, o=cwo: ve.tensor_scalar(
                        o, tk[:, 24:32], tk[:, 10:11], None, op0=ALU.mult), also=EV2)

            ev3_sn = []
            if kind == "k" and g >= 1:
                P.wait("vector", MM, mm_tr[g - 1])
            for dt in range(16):
                bank = pb[dt % 2]
                if len(ev_sn) >= 2:
                    P.wait("tensor", EV, ev_sn[-2])
                for ct in range(16):
                    P.on("tensor", lambda te, o=bank[:, :],
                         l=w_s[:, ct * H + dt * 128:ct * H + (dt + 1) * 128],
                         r=hbuf[:, ct * GT:(ct + 1) * GT],
                         st=(ct == 0), sp=(ct == 15): te.matmul(o, l, r, start=st, stop=sp),
                         inc=MM if ct == 15 else None)
                sl = ew[:, (dt % 2) * 3 * GT:(dt % 2) * 3 * GT + 3 * GT]
                P.wait("vector", MM)
                P.on("vector", lambda ve, b=bank, s=sl: ve.tensor_scalar(
                    s[:, 0:GT], b[:, :], 0.0, None, op0=ALU.min), inc=EV2)
                P.wait("scalar", EV2)
                P.on("scalar", lambda se, s=sl: se.activation(
                    s[:, GT:2 * GT], s[:, 0:GT], AF.Exp), inc=SC)
                P.wait("gpsimd", SC)
                P.on("gpsimd", lambda gp, s=sl: gp.tensor_sub(
                    s[:, 2 * GT:3 * GT], s[:, GT:2 * GT], s[:, 0:GT]), inc=GP)
                P.wait("vector", GP)
                P.on("vector", lambda ve, b=bank, s=sl, o=sq_s[:, dt * GT:(dt + 1) * GT]:
                     ve.scalar_tensor_tensor(o, b[:, :], 1.0, s[:, 2 * GT:3 * GT],
                                             op0=ALU.mult, op1=ALU.add), inc=EV)
                ev_sn.append(EV.n)
                if kind == "q" and dt >= 1:
                    _nf_rel(dt - 1, sq_s, nrm_s, ev3_sn)
            if kind == "q":
                _nf_rel(15, sq_s, nrm_s, ev3_sn)
            P.drain("vector", EV)
            P.wait("sync", EV)
            if kind == "q":
                if own:
                    P.dma("sync", ST, sqT_d[:, g * GT:g * GT + GT].rearrange(
                        "(dt p) t -> p dt t", p=128),
                        sq_s.rearrange("p (dt t) -> p dt t", t=GT))
            else:
                dst0 = g * GT if own else T + (g - 8) * GT
                P.dma("sync", ST, skT_d[:, dst0:dst0 + GT].rearrange(
                    "(dt p) t -> p dt t", p=128),
                    sq_s.rearrange("p (dt t) -> p dt t", t=GT))
                stg = tokst[:, (g % 2) * 8192:(g % 2 + 1) * 8192]
                P.wait("tensor", EV)
                ev3_t = []
                for dt in range(16):
                    for j in range(4):
                        i = dt * 4 + j
                        bank = pb[2 + i % 2]
                        if len(ev3_t) >= 2:
                            P.wait("tensor", EV3, ev3_t[-2])
                        P.on("tensor", lambda te, dt=dt, j=j,
                             i_=sq_s[:, dt * GT + j * 128:dt * GT + (j + 1) * 128],
                             o=bank[:, 0:64].bitcast(BF16)[:, 0:128]:
                             te.transpose(o, i_, id_bf), inc=MM)
                        P.wait("vector", MM)
                        if g >= 2 and len(ev3_t) == 0:
                            P.wait("vector", ST, st_tok[g - 2])
                        P.on("vector", lambda ve,
                             b=bank[:, 0:64].bitcast(BF16)[:, 0:128],
                             o=stg[:, j * 2048 + dt * 128:j * 2048 + (dt + 1) * 128]:
                             ve.tensor_copy(o, b), inc=EV3)
                        ev3_t.append(EV3.n)
                mm_tr[g] = MM.n
                P.drain("vector", EV3)
                P.wait("sync", EV3)
                P.dma("sync", ST,
                      sk_tok[dst0:dst0 + GT, :].rearrange("(j p) d -> p j d", p=128),
                      stg.rearrange("p (j d) -> p j d", d=2048))
                st_tok[g] = ST.n
            mm_sn[g] = MM.n
            P.wait("vector", ST)
        P.reset(base)

    if "A2" in active:
        P.barrier(BAR)
        proj_pass("q")
        nrmA = P.salloc("a2_nrmA", (128, 64), BF16)
        P.drain("vector", EV3)
        P.wait("sync", EV3)
        P.dma("sync", ST, relscr[:, :], rel_acc[:, :])
        P.wait("sync", ST)
        P.dma("sync", LD, rows[0:1, 0:64], relscr[:, :].transpose([1, 0]),
              allow_slow_non_contiguous=True)
        r3 = rows[0:1, 0:64].rearrange("p (h m) -> p h m", m=4)
        mx = rows[0:1, 64 + 16:64 + 32]
        sm = rows[0:1, 96:112]
        P.wait("vector", LD)
        ch("vector", lambda ve: ve.tensor_scalar(rows[0:1, 0:64], rows[0:1, 0:64],
                                                 1.0 / S_, None, op0=ALU.mult))
        ch("vector", lambda ve: ve.tensor_reduce(mx, r3, axis=AX.X, op=ALU.max))
        ch("vector", lambda ve: ve.tensor_tensor(
            r3, r3, mx.unsqueeze(2).to_broadcast((1, 16, 4)),
            op=ALU.subtract), also=EV2)
        P.wait("scalar", EV2)
        P.on("scalar", lambda se: se.activation(rows[0:1, 0:64], rows[0:1, 0:64], AF.Exp),
             inc=SC)
        P.wait("vector", SC)
        ch("vector", lambda ve: ve.tensor_reduce(sm, r3, axis=AX.X, op=ALU.add))
        ch("vector", lambda ve: ve.reciprocal(rows[0:1, 112:128], sm))
        ch("vector", lambda ve: ve.tensor_tensor(
            r3, r3, rows[0:1, 112:128].unsqueeze(2).to_broadcast((1, 16, 4)),
            op=ALU.mult), also=EV)
        P.dma("gpsimd", LDW, nrmA[:, :], normcat[:, :])
        P.wait("tensor", LDW)
        for h in range(16):
            P.on("tensor", lambda te, h=h: te.matmul(pb[6][0:1, 0:4], ones_col,
                                                     nrmA[:, h * 4:(h + 1) * 4],
                                                     start=(h == 0), stop=(h == 15)),
                 inc=MM if h == 15 else None)
        P.wait("vector", MM)
        ch("vector", lambda ve: ve.tensor_copy(rows[0:1, 128:132], pb[6][0:1, 0:4]))
        ch("vector", lambda ve: ve.reduce_sum(rows[0:1, 132:133], rows[0:1, 128:132],
                                              axis=AX.X))
        ch("vector", lambda ve: ve.tensor_scalar(rows[0:1, 136:140], rows[0:1, 128:132],
                                                 EPS, None, op0=ALU.is_ge))
        ch("vector", lambda ve: ve.tensor_scalar(rows[0:1, 64:65], rows[0:1, 132:133],
                                                 EPS, None, op0=ALU.is_ge))
        ch("vector", lambda ve: ve.tensor_tensor(
            r3, r3, rows[0:1, 136:140].unsqueeze(1).to_broadcast((1, 16, 4)),
            op=ALU.mult))
        ch("vector", lambda ve: ve.tensor_copy(wrow_b, rows[0:1, 0:65]), also=EV)
        P.wait("scalar", LD, CONST_LD)
        P.on("scalar", lambda se: se.activation(rows[0:1, 144:160], mg_row, AF.Sigmoid),
             inc=SC)
        P.wait("vector", SC)
        ch("vector", lambda ve: ve.tensor_copy(grow_b, rows[0:1, 144:160]))
        P.drain("vector", EV)
        P.wait("tensor", EV)
        P.on("tensor", lambda te: te.matmul(pb[6][:, 0:65], ones_row, wrow_b,
                                            start=True, stop=True))
        P.on("tensor", lambda te: te.matmul(pb[7][:, 0:16], ones_row, grow_b,
                                            start=True, stop=True), inc=MM)
        P.wait("vector", MM)
        P.on("vector", lambda ve: ve.tensor_copy(wcols, pb[6][:, 0:64]))
        P.on("vector", lambda ve: ve.tensor_copy(zf_col, pb[6][:, 64:65]))
        P.on("vector", lambda ve: ve.tensor_copy(gcols_f, pb[7][:, 0:16]), inc=EV)
        P.barrier(BAR)
        dump("sqT", sqT_d[:, :])
        dump("hiT", hiT_cm[:, :])
        if "cw" in dbg:
            P.dma("sync", ST, dbg["cw"][:, :], cw_all[:, :])
        if "wrow" in dbg:
            P.dma("sync", ST, dbg["wrow"][0:1, 0:96], wko[0:1, 0:96])
            P.dma("sync", ST, dbg["wrow"][0:1, 96:160], rows[0:1, 0:64])
        if "relo" in dbg:
            P.dma("sync", ST, dbg["relo"][:, :], relscr[:, :])

    if "B1" in active:
        P.barrier(BAR)
        proj_pass("k")
        P.barrier(BAR)
        dump("skT", skT_d[:, :])

    if "B2" in active:
        P.barrier(BAR)
        base = P.mark()
        wv_s = P.salloc("b2_wv", (128, 16 * H), BF16)
        hiT_s = P.salloc("b2_hiT", (128, 2 * 16 * GT), BF16)
        vst = P.salloc("b2_vst", (128, 2 * H), BF16)
        P.wait("sync", ST)
        for ct in range(16):
            P.dma("gpsimd", LDW, wv_s[:, ct * H:(ct + 1) * H], wvT[ct * 128:(ct + 1) * 128, :])
        P.wait("tensor", LDW)
        mm_sn = {}
        ev_sn2 = []
        st_sn2 = []
        for g in range(16):
            own = g < 8
            t0 = (g * GT) if own else (T + (g - 8) * GT)
            hbuf = hiT_s[:, (g % 2) * 16 * GT:(g % 2 + 1) * 16 * GT]
            if g >= 2:
                P.wait("sync", MM, mm_sn[g - 2])
            P.dma("sync", LD, hbuf.rearrange("p (ct t) -> p ct t", t=GT),
                  hiT_cm[:, t0:t0 + GT].rearrange("(ct p) t -> p ct t", p=128))
            P.wait("tensor", LD)
            for ts_ in range(4):
                vbuf = vst[:, (ts_ % 2) * H:(ts_ % 2 + 1) * H]
                if len(st_sn2) >= 2:
                    P.wait("vector", ST, st_sn2[-2])
                for oc in range(4):
                    bank = pb[oc % 2]
                    if len(ev_sn2) >= 2:
                        P.wait("tensor", EV, ev_sn2[-2])
                    for ct in range(16):
                        P.on("tensor", lambda te, o=bank[:, :],
                             l=hbuf[:, ct * GT + ts_ * 128:ct * GT + (ts_ + 1) * 128],
                             r=wv_s[:, ct * H + oc * 512:ct * H + (oc + 1) * 512],
                             st=(ct == 0), sp=(ct == 15):
                             te.matmul(o, l, r, start=st, stop=sp),
                             inc=MM if ct == 15 else None)
                    P.wait("vector", MM)
                    P.on("vector", lambda ve, b=bank, o=vbuf[:, oc * 512:(oc + 1) * 512]:
                         ve.tensor_copy(o, b[:, :]), inc=EV)
                    ev_sn2.append(EV.n)
                P.drain("vector", EV)
                P.wait("sync", EV)
                P.dma("sync", ST, v_tok[t0 + ts_ * 128:t0 + (ts_ + 1) * 128, :], vbuf)
                st_sn2.append(ST.n)
            mm_sn[g] = MM.n
        P.reset(base)
        P.barrier(BAR)
        dump("vtk", v_tok[:, :])

    if "P" in active:
        P.barrier(BAR)
        base = P.mark()
        kc_s = P.salloc("p_kc", (128, 2 * H), BF16)
        vc_s = P.salloc("p_vc", (128, 2 * H), BF16)
        P.wait("sync", ST)
        ld_sn = {}
        mm_sn = {}
        ev_sn = {}
        for cc in range(32):
            par = cc % 2
            kb = kc_s[:, par * H:(par + 1) * H]
            vcb = vc_s[:, par * H:(par + 1) * H]
            if cc >= 2:
                P.wait("sync", MM, mm_sn[cc - 2])
            P.dma("sync", LD, kb, sk_tok[T + cc * 128:T + (cc + 1) * 128, :])
            P.dma("sync", LD, vcb, v_tok[T + cc * 128:T + (cc + 1) * 128, :])
            ld_sn[cc] = LD.n
            P.wait("tensor", LD, ld_sn[cc])
            if cc >= 1:
                P.wait("tensor", EV, ev_sn[cc - 1])
            for h in range(NH):
                b4, hh = h // 4, h % 4
                P.on("tensor", lambda te, o=pb[b4][:, hh * 128:(hh + 1) * 128],
                     l=kb[:, h * 128:(h + 1) * 128], r=vcb[:, h * 128:(h + 1) * 128],
                     st=(hh == 0): te.matmul(o, l, r, start=st, stop=True,
                                             skip_group_check=(hh != 0)),
                     inc=MM if hh == 3 else None)
                P.on("tensor", lambda te,
                     o=pb[4][:, par * 16 + h:par * 16 + h + 1],
                     l=kb[:, h * 128:(h + 1) * 128], st=(h == 0):
                     te.matmul(o, l, ones_col, start=st, stop=True,
                               skip_group_check=not st),
                     inc=MM if h == NH - 1 else None)
            mm_sn[cc] = MM.n
            P.wait("vector", MM, mm_sn[cc])
            for b4 in range(4):
                if cc == 0:
                    P.on("vector", lambda ve, s=S_fS[:, b4 * 512:(b4 + 1) * 512],
                         b=pb[b4]: ve.tensor_copy(s, b[:, 0:512]))
                else:
                    P.on("vector", lambda ve, s=S_fS[:, b4 * 512:(b4 + 1) * 512],
                         b=pb[b4]: ve.tensor_add(s, s, b[:, 0:512]))
            if cc == 0:
                P.on("vector", lambda ve: ve.tensor_copy(S_fz[:, :], pb[4][:, 0:16]),
                     inc=EV)
            else:
                P.on("vector", lambda ve, par=par: ve.tensor_add(
                    S_fz[:, :], S_fz[:, :], pb[4][:, par * 16:par * 16 + 16]), inc=EV)
            ev_sn[cc] = EV.n
        P.wait("vector", EV, ev_sn[31])
        ch("vector", lambda ve: ve.tensor_scalar(
            S_fS[:, 0:1024], S_fS[:, 0:1024], premask[:, 0:1], None, op0=ALU.mult))
        ch("vector", lambda ve: ve.tensor_scalar(
            S_fS[:, 1024:2048], S_fS[:, 1024:2048], premask[:, 0:1], None,
            op0=ALU.mult))
        ch("vector", lambda ve: ve.tensor_scalar(
            S_fz[:, :], S_fz[:, :], premask[:, 0:1], None, op0=ALU.mult))
        P.drain("vector", EV)
        P.wait("scalar", EV)
        P.on("scalar", lambda se: se.copy(S_bS[:, 0:1024], S_fS[:, 0:1024]))
        P.on("scalar", lambda se: se.copy(S_bS[:, 1024:2048], S_fS[:, 1024:2048]))
        P.on("scalar", lambda se: se.copy(S_bz[:, :], S_fz[:, :]), inc=SC)
        P.reset(base)
        P.barrier(BAR)
        dump("S0", S_fS[:, :])

    if "B3a" in active:
        P.barrier(BAR)
        base = P.mark()
        mem_s = P.salloc("b3_mem", (128, NH * 512), BF16)
        Mh_s = P.salloc("b3_Mh", (128, NH * 128), BF16)
        nrm_s = P.salloc("b3_nrm", (128, NH * 4), BF16)
        eg1_s = P.salloc("b3_eg1", (128, 16 * 512), BF16)
        eg1b_s = P.salloc("b3_eg1b", (128, 4), F32)
        eg2_s = P.salloc("b3_eg2", (128, 4), BF16)
        eg2b_s = P.salloc("b3_eg2b", (1, 1), F32)
        sq_g = P.salloc("b3_sq", (128, 16 * GT), BF16)
        sk_g = P.salloc("b3_sk", (128, 16 * GT), BF16)
        v_g = P.salloc("b3_v", (128, 2 * 4 * H), BF16)
        skc_g = P.salloc("b3_skc", (128, 2 * 8192), BF16)
        coaT = P.salloc("b3_coaT", (128, 16 * GT), BF16)
        L1T = P.salloc("b3_L1T", (128, 4 * GT), BF16)
        otK = P.salloc("b3_otK", (128, 2 * H), BF16)
        nf_all = P.salloc("b3_nf", (128, 256), F32)
        ucp_all = P.salloc("b3_ucp", (128, 64), F32)
        exr_row = P.salloc("b3_exr", (128, 512), F32)
        exc_s = P.salloc("b3_exc", (128, 8), F32)
        sh1 = P.salloc("b3_sh1", (128, 2048), BF16)
        mscb = P.salloc("b3_mscb", (128, 2048), BF16)
        locb = P.salloc("b3_locb", (128, 2048), BF16)
        mtmp = P.salloc("b3_mtmp", (128, 1024), BF16)
        igcols = P.salloc("b3_ig", (128, 16), F32)
        u4a = P.salloc("b3_u4a", (128, 2 * 64), F32)
        ucm_a = P.salloc("b3_ucm", (128, 16), F32)
        dcol = P.salloc("b3_dc", (128, 32), F32)
        epsc = P.salloc("b3_eps", (128, 1), F32)
        triu4 = P.salloc("b3_tr4", (128, 512), BF16)

        P.wait("sync", ST)
        P.dma("gpsimd", LDW, mem_s[:, :], memcat[:, :])
        P.on("vector", lambda ve: ve.tensor_scalar(
            epsc[:, :], zf_col, 0.0, EPS, op0=ALU.mult, op1=ALU.add))
        P.dma("gpsimd", LDW, nrm_s[:, :], normcat[:, :])
        for ct in range(16):
            P.dma("gpsimd", LDW, eg1_s[:, ct * 512:(ct + 1) * 512],
                  eg1T[ct * 128:(ct + 1) * 128, :])
        P.dma("gpsimd", LDW, eg1b_s[:, :], eg1b.rearrange("(a p) -> p a", p=128),
              allow_slow_non_contiguous=True)
        P.dma("gpsimd", LDW, eg2_s[:, :], eg2T.rearrange("(a p) -> p a", p=128),
              allow_slow_non_contiguous=True)
        P.dma("gpsimd", LDW, eg2b_s[:, :], eg2b[:, :])
        ch("vector", lambda ve: ve.tensor_scalar(
            igcols[:, :], gcols_f, -1.0, 1.0, op0=ALU.mult, op1=ALU.add))
        for j in range(4):
            ch("vector", lambda ve, j=j: ve.tensor_copy(
                triu4[:, j * 128:(j + 1) * 128], triu))
        P.drain("vector", EV)
        P.wait("gpsimd", LDW)
        for h in range(NH):
            m = mem_s[:, h * 512:(h + 1) * 512]
            o = Mh_s[:, h * 128:(h + 1) * 128]
            ch("gpsimd", lambda gp, m=m, o=o: gp.tensor_add(o, m[:, 0:128], m[:, 128:256]))
            P.on("gpsimd", lambda gp, m=m: gp.tensor_add(L1T[:, 0:128], m[:, 256:384],
                                                         m[:, 384:512]), inc=CH["gpsimd"])
            ch("gpsimd", lambda gp, o=o: gp.tensor_add(o, o, L1T[:, 0:128]))
        P.drain("gpsimd", GP)
        P.wait("tensor", GP)
        P.wait("tensor", EV)
        P.wait("tensor", LDW)

        def b_u4(c):
            u4p = u4a[:, (c % 2) * 64:(c % 2 + 1) * 64]
            nfv = nf_all.rearrange("p (hh s m) -> p hh s m", s=4, m=4)[:, :, c, :]
            u43 = u4p.rearrange("p (hh m) -> p hh m", m=4)
            ch("vector", lambda ve, a=nfv, o=u43: ve.tensor_tensor(
                o, a, epsc.unsqueeze(1).to_broadcast((128, 16, 4)), op=ALU.max))
            ch("vector", lambda ve, o=u4p: ve.reciprocal(o, o))
            ch("vector", lambda ve, o=u4p: ve.tensor_tensor(o, o, wcols, op=ALU.mult))
            ch("vector", lambda ve, o=u4p, c=c: ve.tensor_scalar(
                o, o, exc_s[:, c:c + 1], None, op0=ALU.mult))
            ucpv = ucp_all.rearrange("p (hh s) -> p hh s", s=4)[:, :, c:c + 1]
            ch("vector", lambda ve, u=ucpv, c=c: ve.tensor_tensor(
                ucm_a[:, 0:16].rearrange("p (hh o) -> p hh o", o=1),
                u, exc_s[:, 4 + c:5 + c].unsqueeze(1).to_broadcast((128, 16, 1)),
                op=ALU.mult))
            ch("vector", lambda ve, o=u43: ve.tensor_tensor(
                o, o, ucm_a[:, 0:16].rearrange("p (hh o) -> p hh o", o=1)
                .to_broadcast((128, 16, 4)), op=ALU.add))
            ch("vector", lambda ve, o=u43: ve.tensor_tensor(
                o, o, gcols_f.rearrange("p (hh o) -> p hh o", o=1)
                .to_broadcast((128, 16, 4)), op=ALU.mult))

        last_spill = {0: 0, 1: 0}
        sc_pb03 = 0
        gp_read_done = 0
        gp_lane_free = 0
        u4free = {0: 0, 1: 0}
        sc_H = 0
        mm_g_end = {}
        uv3 = ucp_all.rearrange("p (h s) -> p h s", s=4)
        for g in range(NG):
            t0 = g * GT
            vb = v_g[:, (g % 2) * 4 * H:(g % 2 + 1) * 4 * H]
            skcb = skc_g[:, (g % 2) * 8192:(g % 2 + 1) * 8192]
            if g >= 2:
                P.wait("sync", MM, mm_g_end[g - 2])
            for c in range(4):
                P.dma("sync", LD, vb[:, c * H:(c + 1) * H],
                      v_tok[t0 + c * 128:t0 + (c + 1) * 128, :])
            ld_v = LD.n
            for c in range(4):
                P.dma("sync", LD, skcb[:, c * H:(c + 1) * H],
                      sk_tok[t0 + c * 128:t0 + (c + 1) * 128, :])
            ld_skc = LD.n
            P.wait("sync", MM)
            P.dma("sync", LD, sq_g.rearrange("p (dt t) -> p dt t", t=GT),
                  sqT_d[:, t0:t0 + GT].rearrange("(dt p) t -> p dt t", p=128))
            P.dma("sync", LD, sk_g.rearrange("p (dt t) -> p dt t", t=GT),
                  skT_d[:, t0:t0 + GT].rearrange("(dt p) t -> p dt t", p=128))
            ld_qk = LD.n

            P.wait("tensor", LD, ld_qk)
            P.wait("tensor", EV)
            P.wait("tensor", SC)
            P.wait("tensor", GP)
            first = True
            for sb in range(4):
                for h in range(NH):
                    o = pb[4][:, (h * 4 + sb) * 4:(h * 4 + sb) * 4 + 4]
                    l = sq_g[:, h * GT + sb * 128:h * GT + (sb + 1) * 128]
                    r = nrm_s[:, h * 4:(h + 1) * 4]
                    P.on("tensor", lambda te, o=o, l=l, r=r, st=first:
                         te.matmul(o, l, r, start=st, stop=True,
                                   skip_group_check=not st),
                         inc=MM if (sb == 3 and h == NH - 1) else None)
                    first = False
            P.wait("vector", MM)
            P.wait("vector", GP)
            ch("vector", lambda ve: ve.tensor_copy(nf_all[:, :], pb[4][:, 0:256]))
            ch("vector", lambda ve: ve.tensor_reduce(
                ucp_all[:, :], nf_all.rearrange("p (a m) -> p a m", m=4),
                axis=AX.X, op=ALU.add))
            ch("vector", lambda ve: ve.tensor_scalar(
                ucp_all[:, :], ucp_all[:, :], EPS, None, op0=ALU.max))
            ch("vector", lambda ve: ve.reciprocal(ucp_all[:, :], ucp_all[:, :]))
            ch("vector", lambda ve: ve.tensor_scalar(
                ucp_all[:, :], ucp_all[:, :], zf_col, None, op0=ALU.mult))
            chv_ucp = CH["vector"].n

            sc_coa = []
            tr_rot = 0
            cost_ev = {}
            for sb in range(4):
                if sb >= 1:
                    P.wait("tensor", EV, cost_ev[sb - 1])
                for b4 in range(4):
                    for hh in range(4):
                        h = b4 * 4 + hh
                        o = pb[b4][:, hh * 128:(hh + 1) * 128]
                        l = sq_g[:, h * GT + sb * 128:h * GT + (sb + 1) * 128]
                        r = Mh_s[:, h * 128:(h + 1) * 128]
                        P.on("tensor", lambda te, o=o, l=l, r=r, st=(hh == 0):
                             te.matmul(o, l, r, start=st, stop=True,
                                       skip_group_check=(hh != 0)),
                             inc=MM if hh == 3 else None)
                if sb == 0:
                    P.wait("vector", CH["vector"], chv_ucp)
                P.wait("vector", MM)
                for b4 in range(4):
                    src = pb[b4][:, 0:512].rearrange("p (h e) -> p h e", e=128)
                    dst = sh1[:, b4 * 512:(b4 + 1) * 512].rearrange(
                        "p (h e) -> p h e", e=128)
                    uv = uv3[:, b4 * 4:(b4 + 1) * 4, sb:sb + 1]
                    P.on("vector", lambda ve, o=dst, a=src, u=uv: ve.tensor_tensor(
                        o, a, u.to_broadcast((128, 4, 128)), op=ALU.mult),
                         inc=EV if b4 == 3 else None)
                cost_ev[sb] = EV.n
                P.wait("tensor", EV, cost_ev[sb])
                for h in range(NH):
                    bank = pb[5 + tr_rot % 2]
                    tr_rot += 1
                    if len(sc_coa) >= 2:
                        P.wait("tensor", SC, sc_coa[-2])
                    P.on("tensor", lambda te, i_=sh1[:, h * 128:(h + 1) * 128],
                         o=bank[:, 0:64].bitcast(BF16)[:, 0:128]:
                         te.transpose(o, i_, id_bf), inc=MM)
                    P.wait("scalar", MM)
                    P.on("scalar", lambda se, h=h, sb=sb,
                         b=bank[:, 0:64].bitcast(BF16)[:, 0:128]:
                         se.copy(coaT[:, h * GT + sb * 128:h * GT + (sb + 1) * 128], b),
                         inc=SC)
                    sc_coa.append(SC.n)

            P.drain("scalar", SC)
            P.wait("tensor", SC)
            P.wait("tensor", EV)
            sc_l1 = []
            for ut in range(4):
                bank = pb[2 + ut % 2]
                if len(sc_l1) >= 2:
                    P.wait("tensor", SC, sc_l1[-2])
                for ct in range(16):
                    P.on("tensor", lambda te, o=bank[:, :],
                         l=eg1_s[:, ct * 512 + ut * 128:ct * 512 + (ut + 1) * 128],
                         r=coaT[:, ct * GT:(ct + 1) * GT],
                         st=(ct == 0), sp=(ct == 15): te.matmul(o, l, r, start=st, stop=sp),
                         inc=MM if ct == 15 else None)
                P.wait("scalar", MM)
                P.on("scalar", lambda se, b=bank, ut=ut: se.activation(
                    L1T[:, ut * 512:(ut + 1) * 512], b[:, :], AF.Relu,
                    bias=eg1b_s[:, ut:ut + 1]), inc=SC)
                sc_l1.append(SC.n)
            P.drain("scalar", SC)
            P.wait("tensor", SC)
            for ut in range(4):
                P.on("tensor", lambda te, ut=ut: te.matmul(
                    pb[4][0:1, :], eg2_s[:, ut:ut + 1], L1T[:, ut * 512:(ut + 1) * 512],
                    start=(ut == 0), stop=(ut == 3)), inc=MM if ut == 3 else None)
            P.wait("scalar", MM)
            P.on("scalar", lambda se: se.activation(exr_row[0:1, :], pb[4][0:1, :],
                                                    AF.Sigmoid, bias=eg2b_s[0:1, 0:1]))
            P.drain("scalar", SC)
            if "exr" in dbg:
                P.wait("sync", SC)
                P.dma("sync", ST, dbg["exr"][0:1, t0:t0 + GT], exr_row[0:1, :])
            P.wait("tensor", SC)
            for sb in range(4):
                P.on("tensor", lambda te, sb=sb: te.transpose(
                    pb[4][:, 64 + sb:64 + sb + 1], exr_row[0:1, sb * 128:(sb + 1) * 128],
                    id_f[0:1, 0:1]), inc=MM if sb == 3 else None)
            P.wait("vector", MM)
            ch("vector", lambda ve: ve.tensor_copy(exc_s[:, 0:4], pb[4][:, 64:68]))
            ch("vector", lambda ve: ve.tensor_scalar(
                exc_s[:, 4:8], exc_s[:, 0:4], -1.0, 1.0, op0=ALU.mult, op1=ALU.add),
                also=EV)
            sc_pb03 = SC.n

            for c in range(4):
                cp = c % 2
                P.wait("vector", SC, u4free[cp])
                b_u4(c)
                chv_u4 = CH["vector"].n
                P.wait("tensor", SC, sc_pb03)
                for h in range(NH):
                    b4, hh = h // 4, h % 4
                    o = pb[b4][:, hh * 128:(hh + 1) * 128]
                    l = sk_g[:, h * GT + c * 128:h * GT + (c + 1) * 128]
                    r = sq_g[:, h * GT + c * 128:h * GT + (c + 1) * 128]
                    P.on("tensor", lambda te, o=o, l=l, r=r, st=(hh == 0):
                         te.matmul(o, l, r, start=st, stop=True,
                                   skip_group_check=(hh != 0)),
                         inc=MM if hh == 3 else None)
                mm_A = MM.n
                P.wait("vector", MM, mm_A)
                for b4 in range(4):
                    P.on("vector", lambda ve, b=pb[b4],
                         o=mscb[:, b4 * 512:(b4 + 1) * 512]:
                         ve.scalar_tensor_tensor(o, b[:, 0:512], 1.0, triu4,
                                                 op0=ALU.mult, op1=ALU.mult),
                         inc=EV if b4 == 3 else None)
                ev_B = EV.n
                P.wait("tensor", EV, ev_B)
                P.wait("tensor", SC, sc_H)
                if c == 0:
                    P.wait("tensor", LD, ld_v)
                for h in range(NH):
                    b4, hh = h // 4, h % 4
                    o = pb[b4][:, hh * 128:(hh + 1) * 128]
                    msct = mscb[:, h * 128:(h + 1) * 128]
                    vct = vb[:, c * H + h * 128:c * H + (h + 1) * 128]
                    sqt = sq_g[:, h * GT + c * 128:h * GT + (c + 1) * 128]
                    P.on("tensor", lambda te, o=o, l=msct, r=vct, st=(hh == 0):
                         te.matmul(o, l, r, start=st, stop=False,
                                   skip_group_check=(hh != 0)))
                    P.on("tensor", lambda te, o=o, l=sqt,
                         S=S_bS[:, h * 128:(h + 1) * 128]:
                         te.matmul(o, l, S, start=False, stop=True,
                                   skip_group_check=True),
                         inc=MM if hh == 3 else None)
                    od = pb[4][:, cp * 16 + h:cp * 16 + h + 1]
                    P.on("tensor", lambda te, o=od, l=msct, st=(h == 0):
                         te.matmul(o, l, ones_col, start=st, stop=False,
                                   skip_group_check=not st))
                    P.on("tensor", lambda te, o=od, l=sqt, z=S_bz[:, h:h + 1]:
                         te.matmul(o, l, z, start=False, stop=True,
                                   skip_group_check=True),
                         inc=MM if h == NH - 1 else None)
                mm_C = MM.n
                P.wait("vector", MM, mm_C)
                ch("vector", lambda ve, cp=cp: ve.tensor_scalar(
                    dcol[:, 0:16], pb[4][:, cp * 16:cp * 16 + 16], EPS, None,
                    op0=ALU.max))
                ch("vector", lambda ve: ve.reciprocal(dcol[:, 16:32], dcol[:, 0:16]))
                ch("vector", lambda ve: ve.tensor_tensor(dcol[:, 0:16], dcol[:, 16:32],
                                                         igcols, op=ALU.mult))
                P.wait("vector", GP, gp_read_done)
                for b4 in range(4):
                    src = pb[b4][:, 0:512].rearrange("p (h e) -> p h e", e=128)
                    dst = locb[:, b4 * 512:(b4 + 1) * 512].rearrange(
                        "p (h e) -> p h e", e=128)
                    u = dcol[:, b4 * 4:(b4 + 1) * 4].unsqueeze(2)
                    f = (lambda ve, o=dst, a=src, u=u: ve.tensor_tensor(
                        o, a, u.to_broadcast((128, 4, 128)), op=ALU.mult))
                    if b4 == 0:
                        ch("vector", f)
                    else:
                        P.on("vector", f, inc=EV if b4 == 3 else None)
                ev_E = EV.n
                P.wait("tensor", EV, ev_E)
                if c == 0:
                    P.wait("tensor", LD, ld_skc)
                for h in range(NH):
                    b4, hh = h // 4, h % 4
                    skct = skcb[:, c * H + h * 128:c * H + (h + 1) * 128]
                    vct = vb[:, c * H + h * 128:c * H + (h + 1) * 128]
                    P.on("tensor", lambda te, o=pb[b4][:, hh * 128:(hh + 1) * 128],
                         l=skct, r=vct, st=(hh == 0):
                         te.matmul(o, l, r, start=st, stop=True,
                                   skip_group_check=(hh != 0)),
                         inc=MM if hh == 3 else None)
                    P.on("tensor", lambda te, o=pb[7][:, cp * 16 + h:cp * 16 + h + 1],
                         l=skct, st=(h == 0):
                         te.matmul(o, l, ones_col, start=st, stop=True,
                                   skip_group_check=not st),
                         inc=MM if h == NH - 1 else None)
                mm_F = MM.n
                P.wait("vector", MM, mm_F)
                P.wait("vector", SC, sc_H)
                for b4 in range(4):
                    P.on("vector", lambda ve, s=S_fS[:, b4 * 512:(b4 + 1) * 512],
                         b=pb[b4]: ve.tensor_add(s, s, b[:, 0:512]))
                P.on("vector", lambda ve, cp=cp: ve.tensor_add(
                    S_fz[:, :], S_fz[:, :], pb[7][:, cp * 16:cp * 16 + 16]), inc=EV)
                ev_G = EV.n
                P.wait("scalar", EV, ev_G)
                P.on("scalar", lambda se: se.copy(S_bS[:, 0:1024], S_fS[:, 0:1024]))
                P.on("scalar", lambda se: se.copy(S_bS[:, 1024:2048], S_fS[:, 1024:2048]))
                P.on("scalar", lambda se: se.copy(S_bz[:, :], S_fz[:, :]))
                P.drain("scalar", SC)
                sc_H = SC.n
                P.wait("scalar", CH["vector"], chv_u4)
                sc_grp = {}
                gp_grp = {}
                for k in range(4):
                    if k == 0:
                        P.wait("tensor", EV, ev_G)
                    else:
                        P.wait("tensor", SC, sc_grp[k - 1])
                    for hh in range(4):
                        h = k * 4 + hh
                        sqt = sq_g[:, h * GT + c * 128:h * GT + (c + 1) * 128]
                        P.on("tensor", lambda te, o=pb[hh][:, :], l=sqt,
                             r=mem_s[:, h * 512:(h + 1) * 512]:
                             te.matmul(o, l, r, start=True, stop=True),
                             inc=MM if hh == 3 else None)
                    mm_af = MM.n
                    P.wait("scalar", MM, mm_af)
                    P.wait("scalar", GP, gp_lane_free)
                    for hh in range(4):
                        h = k * 4 + hh
                        u4base = cp * 64 + h * 4
                        for m in range(4):
                            P.on("scalar", lambda se,
                                 o=sh1[:, hh * 512 + m * 128:hh * 512 + (m + 1) * 128],
                                 b=pb[hh][:, m * 128:(m + 1) * 128],
                                 s=u4a[:, u4base + m:u4base + m + 1]:
                                 se.activation(o, b, AF.Copy, scale=s),
                                 inc=SC if (hh == 3 and m == 3) else None)
                    sc_grp[k] = SC.n
                    P.wait("gpsimd", SC, sc_grp[k])
                    if k == 0:
                        P.wait("gpsimd", EV, ev_E)
                        P.wait("gpsimd", ST, last_spill[cp])
                    for hh in range(4):
                        h = k * 4 + hh
                        P.on("gpsimd", lambda gp, o=mtmp[:, hh * 256:hh * 256 + 128],
                             a=locb[:, h * 128:(h + 1) * 128],
                             b=sh1[:, hh * 512:hh * 512 + 128]:
                             gp.tensor_add(o, a, b))
                    for hh in range(4):
                        P.on("gpsimd", lambda gp,
                             o=mtmp[:, hh * 256 + 128:hh * 256 + 256],
                             a=sh1[:, hh * 512 + 128:hh * 512 + 256],
                             b=sh1[:, hh * 512 + 256:hh * 512 + 384]:
                             gp.tensor_add(o, a, b),
                             inc=CH["gpsimd"] if hh == 3 else None)
                    cgb = CH["gpsimd"].n
                    P.wait("gpsimd", CH["gpsimd"], cgb)
                    for hh in range(4):
                        P.on("gpsimd", lambda gp, o=mtmp[:, hh * 256:hh * 256 + 128],
                             b=mtmp[:, hh * 256 + 128:hh * 256 + 256]:
                             gp.tensor_add(o, o, b),
                             inc=CH["gpsimd"] if hh == 3 else None)
                    cgc = CH["gpsimd"].n
                    P.wait("gpsimd", CH["gpsimd"], cgc)
                    for hh in range(4):
                        h = k * 4 + hh
                        P.on("gpsimd", lambda gp,
                             o=otK[:, cp * H + h * 128:cp * H + (h + 1) * 128],
                             a=mtmp[:, hh * 256:hh * 256 + 128],
                             b=sh1[:, hh * 512 + 384:hh * 512 + 512]:
                             gp.tensor_add(o, a, b), inc=GP if hh == 3 else None)
                    gp_grp[k] = GP.n
                    gp_lane_free = GP.n
                gp_read_done = GP.n
                sc_pb03 = sc_grp[3]
                u4free[cp] = sc_grp[3]
                P.drain("gpsimd", GP)
                P.dma("gpsimd", ST, out_pre[t0 + c * 128:t0 + (c + 1) * 128, :],
                      otK[:, cp * H:(cp + 1) * H])
                last_spill[cp] = ST.n
            mm_g_end[g] = MM.n
        P.reset(base)
        P.barrier(BAR)
        dump("outT", out_pre[:, :])

    if "B3w" in active:
        P.barrier(BAR)
        P.reset(S_BASE)
        base = P.mark()
        wo_s = P.salloc("bw_wo", (128, 16 * H), BF16)
        oT_s = P.salloc("bw_oT", (128, 16 * GT), BF16)
        x_s = P.salloc("bw_x", (128, 2 * H), F32)
        y_s = P.salloc("bw_y", (128, 2 * H), F32)
        P.wait("sync", ST)
        for ct in range(16):
            P.dma("gpsimd", LDW, wo_s[:, ct * H:(ct + 1) * H], woT[ct * 128:(ct + 1) * 128, :])
        P.wait("tensor", LDW)
        ev_sn3 = []
        st_sn3 = []
        for g in range(NG):
            t0 = g * GT
            P.wait("sync", MM)
            for ct in range(16):
                P.dma_tr(LD, oT_s[:, ct * GT:(ct + 1) * GT],
                         out_pre[t0:t0 + GT, ct * 128:(ct + 1) * 128])
            for ts_ in range(4):
                xb = x_s[:, (ts_ % 2) * H:(ts_ % 2 + 1) * H]
                yb = y_s[:, (ts_ % 2) * H:(ts_ % 2 + 1) * H]
                if len(st_sn3) >= 2:
                    P.wait("sync", ST, st_sn3[-2])
                P.dma("sync", LD, xb, x_own[t0 + ts_ * 128:t0 + (ts_ + 1) * 128, :])
                P.wait("tensor", LD)
                for oc in range(4):
                    bank = pb[oc % 2]
                    if len(ev_sn3) >= 2:
                        P.wait("tensor", EV, ev_sn3[-2])
                    for ct in range(16):
                        P.on("tensor", lambda te, o=bank[:, :],
                             l=oT_s[:, ct * GT + ts_ * 128:ct * GT + (ts_ + 1) * 128],
                             r=wo_s[:, ct * H + oc * 512:ct * H + (oc + 1) * 512],
                             st=(ct == 0), sp=(ct == 15):
                             te.matmul(o, l, r, start=st, stop=sp),
                             inc=MM if ct == 15 else None)
                    P.wait("vector", MM)
                    P.on("vector", lambda ve, b=bank, xb=xb, yb=yb, oc=oc: ve.tensor_add(
                        yb[:, oc * 512:(oc + 1) * 512], b[:, :],
                        xb[:, oc * 512:(oc + 1) * 512]), inc=EV)
                    ev_sn3.append(EV.n)
                P.drain("vector", EV)
                P.wait("sync", EV)
                P.dma("sync", ST, out[t0 + ts_ * 128:t0 + (ts_ + 1) * 128, :], yb)
                st_sn3.append(ST.n)
        P.reset(base)

    def ffn_pass(tag, wg_src, wu_src, wd_src, cw_idx, ov=None):
        if ov is None:
            P.barrier(BAR)
        P.reset(S_BASE)
        base = P.mark()
        wg_s = P.salloc(f"{tag}_wg", (128, 16 * F), BF16)
        wu_s = P.salloc(f"{tag}_wu", (128, 16 * F), BF16)
        wd_s = P.salloc(f"{tag}_wd", (128, NFT * H), BF16)
        hiT_s = P.salloc(f"{tag}_hiT", (128, 2 * 16 * GT), BF16)
        g_bf = P.salloc(f"{tag}_g", (128, NFT * GT), BF16)
        s1_s = P.salloc(f"{tag}_s1", (128, 2 * GT), BF16)
        yd_s = P.salloc(f"{tag}_yd", (128, 2 * H), F32)

        if ov is None:
            P.wait("gpsimd", ST)
            P.wait("sync", ST)
        else:
            P.wait("gpsimd", MM, ov["mm_up"])
        for ct in range(16):
            P.dma("gpsimd", LDW, wg_s[:, ct * F:(ct + 1) * F], wg_src[ct * 128:(ct + 1) * 128, :])
            P.dma("gpsimd", LDW, wu_s[:, ct * F:(ct + 1) * F], wu_src[ct * 128:(ct + 1) * 128, :])
        if ov is not None:
            P.wait("gpsimd", MM2, ov["mm2"])
        for ft in range(NFT):
            P.dma("gpsimd", LDW, wd_s[:, ft * H:(ft + 1) * H], wd_src[ft * 128:(ft + 1) * 128, :])
        if ov is not None:
            P.wait("tensor", EV2, ov["ev2"])
        P.wait("tensor", LDW)

        ld_sn = {}
        ev_g = []
        ev_y = []
        st_y = []
        for g in range(NG):
            t0 = g * GT
            hbuf = hiT_s[:, (g % 2) * 16 * GT:(g % 2 + 1) * 16 * GT]
            if g >= 2:
                P.wait("sync", MM, ld_sn[g - 2])
            elif ov is not None:
                P.wait("sync", MM, ov["ld2"][g])
            P.dma("sync", LD, hbuf.rearrange("p (ct t) -> p ct t", t=GT),
                  hiT_cm[:, t0:t0 + GT].rearrange("(ct p) t -> p ct t", p=128))
            P.wait("tensor", LD)
            P.wait("tensor", MM2)
            for ft in range(NFT):
                b1 = pb[ft % 2]
                b2 = pb[2 + ft % 2]
                if len(ev_g) >= 2:
                    P.wait("tensor", EV, ev_g[-2])
                for ct in range(16):
                    P.on("tensor", lambda te, o=b1[:, :],
                         l=wg_s[:, ct * F + ft * 128:ct * F + (ft + 1) * 128],
                         r=hbuf[:, ct * GT:(ct + 1) * GT],
                         st=(ct == 0), sp=(ct == 15): te.matmul(o, l, r, start=st, stop=sp))
                for ct in range(16):
                    P.on("tensor", lambda te, o=b2[:, :],
                         l=wu_s[:, ct * F + ft * 128:ct * F + (ft + 1) * 128],
                         r=hbuf[:, ct * GT:(ct + 1) * GT],
                         st=(ct == 0), sp=(ct == 15): te.matmul(o, l, r, start=st, stop=sp),
                         inc=MM if ct == 15 else None)
                P.wait("scalar", MM)
                P.on("scalar", lambda se, b=b1, ft=ft: se.activation(
                    s1_s[:, (ft % 2) * GT:(ft % 2 + 1) * GT], b[:, :], AF.Silu), inc=SC)
                P.wait("vector", SC)
                P.on("vector", lambda ve, b=b2, ft=ft: ve.tensor_mul(
                    g_bf[:, ft * GT:(ft + 1) * GT],
                    s1_s[:, (ft % 2) * GT:(ft % 2 + 1) * GT], b[:, :]), inc=EV)
                ev_g.append(EV.n)
            P.drain("vector", EV)
            P.wait("tensor", EV)
            for ts_ in range(4):
                yb = yd_s[:, (ts_ % 2) * H:(ts_ % 2 + 1) * H]
                if len(st_y) >= 2:
                    P.wait("vector", ST, st_y[-2])
                    P.wait("scalar", ST, st_y[-2])
                elif ov is not None:
                    P.wait("vector", ST, ov["st"])
                    P.wait("scalar", ST, ov["st"])
                for oc in range(4):
                    bank = pb[4 + oc % 2]
                    if len(ev_y) >= 2:
                        P.wait("tensor", EV2, ev_y[-2])
                    for ft in range(NFT):
                        P.on("tensor", lambda te, o=bank[:, :],
                             l=g_bf[:, ft * GT + ts_ * 128:ft * GT + (ts_ + 1) * 128],
                             r=wd_s[:, ft * H + oc * 512:ft * H + (oc + 1) * 512],
                             st=(ft == 0), sp=(ft == NFT - 1):
                             te.matmul(o, l, r, start=st, stop=sp),
                             inc=MM2 if ft == NFT - 1 else None)
                    ysl = yb[:, oc * 512:(oc + 1) * 512]
                    if cw_idx is None:
                        if oc % 2 == 0:
                            P.wait("vector", MM2)
                            P.on("vector", lambda ve, b=bank, o=ysl: ve.tensor_copy(o, b[:, :]),
                                 inc=EV2)
                        else:
                            P.wait("scalar", MM2)
                            P.on("scalar", lambda se, b=bank, o=ysl: se.copy(o, b[:, :]),
                                 inc=EV2)
                    else:
                        cwc = cw_all[:, (g * 4 + ts_) * 8 + cw_idx:(g * 4 + ts_) * 8 + cw_idx + 1]
                        if oc % 2 == 0:
                            P.wait("vector", MM2)
                            P.on("vector", lambda ve, b=bank, o=ysl, cw=cwc: ve.tensor_scalar(
                                o, b[:, :], cw, None, op0=ALU.mult), inc=EV2)
                        else:
                            P.wait("scalar", MM2)
                            P.on("scalar", lambda se, b=bank, o=ysl, cw=cwc: se.activation(
                                o, b[:, :], AF.Copy, scale=cw), inc=EV2)
                    ev_y.append(EV2.n)
                P.drain("vector", EV2)
                P.drain("scalar", EV2)
                P.wait("gpsimd", EV2)
                if ov is not None and g == 0 and ts_ == 0:
                    P.wait("gpsimd", ST, ov["st"])
                P.dma("gpsimd", ST, out[t0 + ts_ * 128:t0 + (ts_ + 1) * 128, :], yb,
                      accum_op=ALU.add)
                st_y.append(ST.n)
            ld_sn[g] = MM.n
        P.reset(base)
        return {"mm_up": MM.n, "mm2": MM2.n, "ev2": EV2.n, "st": ST.n,
                "ld2": [ld_sn[NG - 2], ld_sn[NG - 1]]}

    if "S0" in active:
        ffn_pass("s0", swg[:, 0:F], swu[:, 0:F], swd[0:F, :], None)
    if "S1" in active:
        ffn_pass("s1", swg[:, F:FS], swu[:, F:FS], swd[F:FS, :], None)
    for e in range(E):
        if f"E{e}" in active:
            ffn_pass(f"e{e}", mwg[e], mwu[e], mwd[e], e)

    P.barrier(BAR)
    P.wait("gpsimd", ST)
    P.wait("sync", ST)

    P.emit()
    return nc, dbg


def _prep_inputs(inputs):
    f32 = np.float32
    bf = ml_dtypes.bfloat16
    x = np.asarray(inputs["L0_attn_hidden_states"], f32)
    tp = lambda a: np.ascontiguousarray(np.asarray(a, f32).T)
    com = {
        "wqT": tp(inputs["L0_attn_w_q"]),
        "wkT": tp(inputs["L0_attn_w_k"]),
        "wvT": tp(inputs["L0_attn_w_v"]),
        "woT": tp(inputs["L0_attn_w_o"]),
        "eg1T": tp(inputs["L0_attn_eg_w1"]),
        "eg1b": np.ascontiguousarray(np.asarray(inputs["L0_attn_eg_b1"], f32)),
        "eg2T": np.ascontiguousarray(np.asarray(inputs["L0_attn_eg_w2"], f32)[0]),
        "eg2b": np.asarray(inputs["L0_attn_eg_b2"], f32).reshape(1, 1),
        "mgate": np.asarray(inputs["L0_attn_memory_gate"], f32).reshape(1, NH),
        "memcat": np.ascontiguousarray(
            np.asarray(inputs["L0_attn_fine_memories"], f32).transpose(2, 1, 0, 3)
            .reshape(HD, NH * 4 * HD)),
        "normcat": np.ascontiguousarray(
            np.asarray(inputs["L0_attn_fine_memory_norms"], f32).transpose(2, 1, 0)
            .reshape(HD, NH * 4)),
        "gwT": tp(inputs["L0_ffn_gate_weight"]),
        "mwg": np.ascontiguousarray(np.asarray(inputs["L0_ffn_w_gate"], f32).transpose(0, 2, 1)),
        "mwu": np.ascontiguousarray(np.asarray(inputs["L0_ffn_w_up"], f32).transpose(0, 2, 1)),
        "mwd": np.ascontiguousarray(np.asarray(inputs["L0_ffn_w_down"], f32).transpose(0, 2, 1)),
        "swg": tp(inputs["L0_ffn_sw_gate"]),
        "swu": tp(inputs["L0_ffn_sw_up"]),
        "swd": tp(inputs["L0_ffn_sw_down"]),
        "c_id_f": np.eye(128, dtype=f32),
        "c_triu_f": np.triu(np.ones((128, 128), dtype=f32)),
        "c_ones_f": np.ones((128, 130), dtype=f32),
    }
    in_maps = []
    for c in range(8):
        b, half = c // 2, c % 2
        m = dict(com)
        m["x_own"] = np.ascontiguousarray(x[b, half * T:(half + 1) * T])
        m["x_oth"] = np.ascontiguousarray(x[b, (1 - half) * T:(2 - half) * T])
        pm = np.empty((128, 2), f32)
        pm[:, 0] = 1.0 if half == 1 else 0.0
        pm[:, 1] = 1e-5
        m["c_premask"] = pm
        in_maps.append(m)
    return in_maps


def kernel(**inputs):
    nc, _ = build()
    in_maps = _prep_inputs(inputs)
    res = run_bass_kernel_spmd(nc, in_maps, core_ids=list(range(8)))
    outp = np.empty((B_, S_, H), np.float32)
    for c in range(8):
        b, half = c // 2, c % 2
        outp[b, half * T:(half + 1) * T] = res.results[c]["out"]
    return outp
